# version 05
# baseline (speedup 1.0000x reference)
import math
import traceback
import numpy as np
import concourse.bass as bass
import concourse.mybir as mybir
from concourse.bass_utils import run_bass_kernel_spmd
from contextlib import ExitStack

F32 = mybir.dt.float32
BF16 = mybir.dt.bfloat16
AF = mybir.ActivationFunctionType
ALU = mybir.AluOpType
AX = mybir.AxisListType

NCORES = 8
D = 1024
S = 4096
NT = 2048
NH = 16
HD = 64
DQ = 512
DKV = 512
DH = 4096
NE = 8
SCALE = math.sqrt(NH + DQ + DKV)
EPS = 1e-5

ENGINES = ("sync", "vector", "scalar", "tensor", "gpsimd")
SUB = "full"
SUBE = "full"


class Sem:
    __slots__ = ("h", "n", "name")

    def __init__(self, h, name):
        self.h = h
        self.n = 0
        self.name = name


class Ctx:

    def __init__(self, nc, stack):
        self.nc = nc
        self.stack = stack
        self.ops = {e: [] for e in ENGINES}

    def sem(self, name):
        return Sem(self.stack.enter_context(self.nc.semaphore(name)), name)

    def sbuf(self, name, shape, dtype, stack=None, side=None):
        return (stack or self.stack).enter_context(
            self.nc.sbuf_tensor(name, shape, dtype, side=side))

    def psum(self, name, shape, dtype=F32):
        return self.stack.enter_context(self.nc.psum_tensor(name, shape, dtype))

    def op(self, eng, meth, *args, inc=None, **kw):
        incs = []
        if inc:
            for s, k in inc:
                s.n += k
                incs.append((s, k))
        frs = traceback.extract_stack(limit=5)[:-1]
        loc = ">".join(f"{f.name}:{f.lineno}" for f in frs[-3:])

        def thunk(e, _m=meth, _a=args, _kw=kw, _incs=incs, _loc=loc):
            instr = getattr(e, _m)(*_a, **_kw)
            try:
                instr.annotate(_loc)
            except Exception:
                pass
            for s, k in _incs:
                instr = instr.then_inc(s.h, k)
            return instr

        self.ops[eng].append(thunk)

    def wait(self, eng, sem, val=None):
        v = sem.n if val is None else val
        self.ops[eng].append(lambda e, _s=sem, _v=v: e.wait_ge(_s.h, _v))

    def dma(self, eng, out, in_, sem=None):
        inc = [(sem, 16)] if sem is not None else None
        self.op(eng, "dma_start", inc=inc, out=out, in_=in_)

    def replay(self):
        with self.nc.Block() as block:
            for e in ENGINES:
                getattr(block, e)(lambda eng, _e=e: [f(eng) for f in self.ops[_e]])


class Bank:

    def __init__(self, ctx, t, base=0, width=512):
        self.t = t
        self.base = base
        self.width = width
        self.guard = None
        self.ctx = ctx

    def mm_wait(self, eng="tensor"):
        if self.guard is not None:
            self.ctx.wait(eng, self.guard[0], self.guard[1])

    def set_guard(self, sem):
        self.guard = (sem, sem.n)

    def ap(self, p=128, n=None, off=0, p0=0):
        n = self.width if n is None else n
        return self.t[p0:p0 + p, self.base + off:self.base + off + n]


def build(debug=(), max_phase=99):
    nc = bass.Bass("TRN2", target_bir_lowering=False, debug=False,
                   detect_race_conditions=False)
    stack = ExitStack()
    ctx = Ctx(nc, stack)

    shapes = {
        "x_phA": (8, 128, S), "x_own": (D, NT),
        "cosk": (D, S), "sink": (D, S), "mask8": (8, 1),
        "a_wdq": (D, DQ), "a_bdq": (128, 4), "a_wuq": (DQ, D), "a_buq": (128, 8),
        "a_wqr": (DQ, D), "a_bqr": (128, 8), "a_wkr": (D, D), "a_bkr": (128, 8),
        "a_wdkv": (D, DKV), "a_bdkv": (128, 4), "a_wuk": (DKV, D), "a_buk": (128, 8),
        "a_wuv": (DKV, D), "a_buv": (1, D), "ident": (128, 128),
        "a_wo": (D, D), "a_bo": (128, 8), "perm": (128, 128),
        "f_w12x": (32, 128, 2048), "f_s1": (128, 32), "f_s2": (128, 32),
        "f_h1x": (32, 128, DH), "f_sh1": (128, 32),
        "f_h2x": (32, 128, DH), "f_sh2": (128, 32),
        "f_w3x": (8, 128, DH), "f_s3": (128, 8),
        "r_ln1w": (128, 8), "r_ln1b": (128, 8), "r_ln2w": (128, 8), "r_ln2b": (128, 8),
        "r_decay": (128, 8), "r_decay_row": (1, D), "r_first": (128, 8),
        "r_mixk": (128, 8), "r_mixv": (128, 8), "r_mixr": (128, 8),
        "r_wkvrx": (8, 128, 3 * D), "r_swox": (8, 128, 2 * D),
        "r_fmixk": (128, 8), "r_fmixr": (128, 8),
        "r_fwkx": (32, 128, D), "r_fwvrx": (8, 128, DH + D),
        "m_wgx": (128, 64), "m_bgx": (128, 128),
        "m_w1x": (NE, 32, 128, D), "m_b1x": (128, NE * 32),
        "m_w2x": (NE, 8, 128, DH), "m_b2x": (128, NE * 8),
        "iota_rev": (1, 512),
    }
    class LazyInp(dict):
        def __missing__(self, k):
            ap = nc.dram_tensor(k, list(shapes[k]), F32, kind="ExternalInput").ap()
            self[k] = ap
            return ap
    inp = LazyInp()
    out = nc.dram_tensor("out", [8, 128, NT], F32, kind="ExternalOutput").ap()

    dbg = {}

    def dbg_out(name, shape):
        t = nc.dram_tensor("dbg_" + name, list(shape), F32, kind="ExternalOutput").ap()
        dbg[name] = t
        return t

    with stack:
        build_body(ctx, inp, out, dbg_out, debug, max_phase)
        ctx.replay()
    nc._used_inputs = set(inp.keys())
    return nc


def build_body(ctx, inp, out, dbg_out, debug, max_phase):
    REG = [ctx.psum(f"reg{i}", [128, 1024], F32) for i in range(3)]
    PSA = ctx.psum("psa", [128, 512], F32)
    PSD = ctx.psum("psd", [128, 512], F32)
    B = [Bank(ctx, REG[i // 2], (i % 2) * 512) for i in range(6)] + \
        [Bank(ctx, PSA), Bank(ctx, PSD)]
    s_ld = ctx.sem("s_ld")
    s_xd = ctx.sem("s_xd")
    s_gd = ctx.sem("s_gd")
    s_mm = ctx.sem("s_mm")
    s_ve = ctx.sem("s_ve")
    s_sc = ctx.sem("s_sc")
    s_st = ctx.sem("s_st")
    s_gp = ctx.sem("s_gp")

    ones_col = ctx.sbuf("ones_col", [1, 128], F32)
    ones128 = ctx.sbuf("ones128", [128, 1], F32)
    ones128b = ctx.sbuf("ones128b", [128, 1], BF16)
    ctx.op("vector", "memset", ones_col[:], 1.0)
    ctx.op("vector", "memset", ones128b[:], 1.0)
    ctx.op("vector", "memset", ones128[:], 1.0, inc=[(s_ve, 1)])

    smalls = {}
    for nm in ("a_bdq", "a_bdkv", "a_buk", "a_bkr", "a_buq", "a_bqr", "a_bo",
               "r_ln1w", "r_ln1b", "r_ln2w", "r_ln2b", "r_decay",
               "r_first", "r_mixk", "r_mixv", "r_mixr", "r_fmixk", "r_fmixr",
               "f_s1", "f_s2", "f_sh1", "f_sh2", "f_s3", "mask8"):
        t = ctx.sbuf(nm + "_s", list(inp[nm].shape), F32)
        ctx.dma("sync", t[:], inp[nm][:], s_ld)
        smalls[nm] = t

    def load_small(nm, stack):
        t = ctx.sbuf(nm + "_s", list(inp[nm].shape), F32, stack=stack)
        ctx.dma("sync", t[:], inp[nm][:], s_ld)
        smalls[nm] = t
        return t

    wstage = ctx.sbuf("wstage", [128, 1024], F32)
    wstage_guard = [None]

    def load_weight_bf16(name, ap, K, M, stack=None, eng="sync", side=None):
        wb = ctx.sbuf(name + "_b", [128, (K // 128) * M], BF16, stack=stack, side=side)
        piece = max(1, 1024 // M)
        for k0 in range(0, K // 128, piece):
            kk = min(piece, K // 128 - k0)
            if wstage_guard[0] is not None:
                ctx.wait(eng, s_ve, wstage_guard[0])
            src = ap.rearrange("(k p) m -> p k m", p=128)[:, k0:k0 + kk, :]
            ctx.dma(eng, wstage[:, 0:kk * M].rearrange("p (k m) -> p k m", k=kk) if kk > 1 else wstage[:, 0:M], src if kk > 1 else ap[k0 * 128:(k0 + 1) * 128, :], s_ld)
            ctx.wait("vector", s_ld)
            ctx.op("vector", "tensor_copy", inc=[(s_ve, 1)],
                   out=wb[:, k0 * M:(k0 + kk) * M], in_=wstage[:, 0:kk * M])
            wstage_guard[0] = s_ve.n
        return wb

    TCH = 512
    NCH = S // TCH
    phA = ExitStack()
    w_dkv = load_weight_bf16("a_wdkv", inp["a_wdkv"], D, DKV, stack=phA, side="right")
    w_dq = load_weight_bf16("a_wdq", inp["a_wdq"], D, DQ, stack=phA, side="right")
    w_uk = load_weight_bf16("a_wuk", inp["a_wuk"], DKV, D, stack=phA, side="right")
    w_kr = load_weight_bf16("a_wkr", inp["a_wkr"], D, D, stack=phA, side="right")
    permb = ctx.sbuf("permb", [128, 128], BF16)
    if wstage_guard[0] is not None:
        ctx.wait("sync", s_ve, wstage_guard[0])
    ctx.dma("sync", wstage[:, 0:128], inp["perm"][:], s_ld)
    ctx.wait("vector", s_ld)
    ctx.op("vector", "tensor_copy", out=permb[:], in_=wstage[:, 0:128], inc=[(s_ve, 1)])
    wstage_guard[0] = s_ve.n

    phAB = ExitStack()
    ctkv = [ctx.sbuf(f"ctkv{f}", [128, S], BF16, stack=phAB) for f in range(4)]
    Kp = [ctx.sbuf(f"Kp{p}", [128, S], BF16, stack=phAB) for p in range(8)]
    phQ = ExitStack()
    ctq = [ctx.sbuf(f"ctq{f}", [128, NT], BF16, stack=phQ) for f in range(4)]

    xf = ctx.sbuf("xf", [128, 8 * TCH], F32, stack=phA, side="right")
    xf_t = [xf[:, f * TCH:(f + 1) * TCH] for f in range(8)]
    xln = ctx.sbuf("xln", [128, 8 * TCH], BF16, stack=phA, side="right")
    xln_t = [xln[:, f * TCH:(f + 1) * TCH] for f in range(8)]
    cosb = [ctx.sbuf("cosb0", [128, TCH], F32, stack=phA, side="right")] * 2
    sinb = [ctx.sbuf("sinb0", [128, TCH], F32, stack=phA, side="right")] * 2
    m1b = [ctx.sbuf(f"m1b{i}", [128, TCH], BF16, stack=phA, side="right") for i in range(2)]
    sqb = [ctx.sbuf("sqb0", [128, TCH], F32, stack=phA, side="right")] * 2
    mub = ctx.sbuf("mub", [128, TCH], F32, stack=phA, side="right")
    rsb = ctx.sbuf("rsb", [128, TCH], F32, stack=phA, side="right")
    tt0 = ctx.sbuf("tt0", [128, TCH], F32, stack=phA, side="right")
    tt1 = ctx.sbuf("tt1", [128, TCH], F32, stack=phA, side="right")
    rowb = ctx.sbuf("rowb", [1, 4 * TCH], F32, stack=phA, side="right")

    s_rope = ctx.sem("s_rope")
    s_cons = ctx.sem("s_cons")

    def rsqrt_row(va_row, rs_row, t1_row):
        ctx.op("vector", "reciprocal", t1_row, va_row, inc=[(s_ve, 1)])
        ctx.wait("scalar", s_ve)
        ctx.op("scalar", "activation", rs_row, t1_row, AF.Sqrt, inc=[(s_sc, 1)])
        ctx.wait("vector", s_sc)
        ctx.op("vector", "tensor_mul", t1_row, rs_row, rs_row)
        ctx.op("vector", "tensor_mul", t1_row, t1_row, va_row)
        ctx.op("vector", "tensor_scalar", t1_row, t1_row, -1.0, 3.0,
               op0=ALU.mult, op1=ALU.add)
        ctx.op("vector", "tensor_mul", t1_row, t1_row, rs_row)
        ctx.op("vector", "tensor_scalar_mul", rs_row, t1_row, 0.5, inc=[(s_ve, 1)])

    def ln_plain(src_t, T, out_t, sxb, sx2b, bcb):
        mu_row = rowb[:, 0:T]
        va_row = rowb[:, TCH:TCH + T]
        rs_row = rowb[:, 2 * TCH:2 * TCH + T]
        t1_row = rowb[:, 3 * TCH:3 * TCH + T]
        sxb.mm_wait()
        for f in range(8):
            ctx.op("tensor", "matmul", sxb.ap(1, T), ones128[:], src_t[f],
                   start=(f == 0), stop=(f == 7), inc=([(s_mm, 1)] if f == 7 else None))
        sx2b.mm_wait()
        for f in range(8):
            if len(sq_guard) >= 1:
                ctx.wait("scalar", s_mm, val=sq_guard[-1])
            ctx.op("scalar", "activation", sqb[f % 2][:, 0:T], src_t[f], AF.Square,
                   inc=[(s_sc, 1)])
            ctx.wait("tensor", s_sc)
            ctx.op("tensor", "matmul", sx2b.ap(1, T), ones128[:], sqb[f % 2][:, 0:T],
                   start=(f == 0), stop=(f == 7), inc=[(s_mm, 1)])
            sq_guard.append(s_mm.n)
        ctx.wait("vector", s_mm)
        ctx.op("vector", "tensor_scalar_mul", mu_row, sxb.ap(1, T), 1.0 / D)
        ctx.op("vector", "tensor_scalar_mul", t1_row, sx2b.ap(1, T), 1.0 / D,
               inc=[(s_ve, 1)])
        sxb.set_guard(s_ve)
        sx2b.set_guard(s_ve)
        ctx.op("vector", "tensor_mul", va_row, mu_row, mu_row)
        ctx.op("vector", "tensor_sub", va_row, t1_row, va_row)
        ctx.op("vector", "tensor_scalar_add", va_row, va_row, EPS)
        rsqrt_row(va_row, rs_row, t1_row)
        bcb.mm_wait()
        sxb.mm_wait()
        ctx.wait("tensor", s_ve)
        ctx.op("tensor", "matmul", bcb.ap(128, T, 0), ones_col[:], mu_row,
               start=True, stop=True)
        ctx.op("tensor", "matmul", sxb.ap(128, T, 0), ones_col[:], rs_row,
               start=True, stop=True, inc=[(s_mm, 1)])
        ctx.wait("scalar", s_mm)
        ctx.op("scalar", "copy", mub[:, 0:T], bcb.ap(128, T, 0))
        ctx.op("scalar", "copy", rsb[:, 0:T], sxb.ap(128, T, 0), inc=[(s_sc, 1)])
        bcb.set_guard(s_sc)
        sxb.set_guard(s_sc)
        ctx.wait("vector", s_sc)
        for f in range(8):
            ctx.op("vector", "tensor_sub", tt0[:, 0:T], src_t[f], mub[:, 0:T])
            ctx.op("vector", "tensor_mul", out_t[f], tt0[:, 0:T], rsb[:, 0:T],
                   inc=([(s_ve, 1)] if f == 7 else None))

    sq_guard = []

    def mm_group(bank, lhs_fn, rhs_fn, kchunks, n):
        bank.mm_wait()
        for k in range(kchunks):
            ctx.op("tensor", "matmul", bank.ap(128, n), lhs_fn(k), rhs_fn(k),
                   start=(k == 0), stop=(k == kchunks - 1),
                   inc=([(s_mm, 1)] if k == kchunks - 1 else None))

    def evac(bank, out_sbuf, n, func=AF.Identity, bias=None, scale=None):
        ctx.wait("scalar", s_mm)
        kw = {}
        if bias is not None:
            kw["bias"] = bias
        if scale is not None:
            kw["scale"] = scale
        ctx.op("scalar", "activation", out_sbuf, bank.ap(128, n), func,
               inc=[(s_sc, 1)], **kw)
        bank.set_guard(s_sc)

    if max_phase >= 1:
        rope_guard = [None, None]
        m1_guard = [None, None]
        for t in range(NCH):
            c0 = t * TCH
            if t > 0:
                ctx.wait("scalar", s_ve)
                ctx.wait("scalar", s_mm)
            ctx.dma("scalar", xf[:].rearrange("p (f t) -> p f t", f=8),
                    inp["x_phA"][:, :, c0:c0 + TCH].transpose([1, 0, 2]), s_xd)
            ctx.wait("tensor", s_xd)
            ctx.wait("vector", s_xd)
            ctx.wait("scalar", s_xd)
            if t == 0:
                ctx.wait("tensor", s_ve)
            ln_plain(xf_t, TCH, xln_t, B[6], B[5], B[7])

            ctx.wait("tensor", s_ve)
            for oc in range(4):
                mm_group(B[oc % 2], lambda k, _o=oc: w_dkv[:, k * DKV + _o * 128:k * DKV + _o * 128 + 128],
                         lambda k: xln_t[k], 8, TCH)
                evac(B[oc % 2], ctkv[oc][:, c0:c0 + TCH], TCH,
                     bias=smalls["a_bdkv"][:, oc:oc + 1])
            if t < NT // TCH:
                for oc in range(4):
                    mm_group(B[oc % 2], lambda k, _o=oc: w_dq[:, k * DQ + _o * 128:k * DQ + _o * 128 + 128],
                             lambda k: xln_t[k], 8, TCH)
                    evac(B[oc % 2], ctq[oc][:, c0:c0 + TCH], TCH,
                         bias=smalls["a_bdq"][:, oc:oc + 1])
            for oc in range(8):
                pb = oc % 2
                if rope_guard[0] is not None:
                    ctx.wait("gpsimd", s_ve, rope_guard[0])
                ctx.dma("gpsimd", cosb[pb][:],
                        inp["cosk"][oc * 128:(oc + 1) * 128, c0:c0 + TCH], s_rope)
                ctx.dma("gpsimd", sinb[pb][:],
                        inp["sink"][oc * 128:(oc + 1) * 128, c0:c0 + TCH], s_rope)
                mm_group(B[2 + pb], lambda k, _o=oc: w_kr[:, k * D + _o * 128:k * D + _o * 128 + 128],
                         lambda k: xln_t[k], 8, TCH)
                if m1_guard[pb] is not None:
                    ctx.wait("scalar", s_ve, m1_guard[pb])
                evac(B[2 + pb], m1b[pb][:], TCH, bias=smalls["a_bkr"][:, oc:oc + 1])
                rb = B[4] if pb == 0 else B[5]
                rb.mm_wait()
                ctx.wait("tensor", s_sc)
                ctx.op("tensor", "matmul", rb.ap(128, TCH), permb[:], m1b[pb][:],
                       start=True, stop=True, inc=[(s_mm, 1)])
                mm_group(B[pb], lambda k, _o=oc: w_uk[:, k * D + _o * 128:k * D + _o * 128 + 128],
                         lambda k: ctkv[k][:, c0:c0 + TCH], 4, TCH)
                ctx.wait("vector", s_mm)
                ctx.wait("vector", s_rope)
                ctx.op("vector", "tensor_mul", tt0[:, 0:TCH], m1b[pb][:], cosb[pb][:])
                ctx.op("vector", "tensor_mul", tt1[:, 0:TCH], rb.ap(128, TCH), sinb[pb][:])
                ctx.op("vector", "tensor_add", tt0[:, 0:TCH], tt0[:, 0:TCH], tt1[:, 0:TCH])
                ctx.op("vector", "scalar_tensor_tensor",
                       out=tt1[:, 0:TCH], in0=B[pb].ap(128, TCH),
                       scalar=smalls["a_buk"][:, oc:oc + 1], in1=tt0[:, 0:TCH],
                       op0=ALU.add, op1=ALU.add)
                ctx.op("vector", "tensor_copy", out=Kp[oc][:, c0:c0 + TCH],
                       in_=tt1[:, 0:TCH], inc=[(s_ve, 1)])
                rb.set_guard(s_ve)
                B[pb].set_guard(s_ve)
                rope_guard[0] = s_ve.n
                m1_guard[pb] = s_ve.n

    if "xln" in debug:
        t = dbg_out("xln", (8, 128, TCH))
        ctx.wait("gpsimd", s_ve)
        for f in range(8):
            ctx.dma("gpsimd", t[f], xln_t[f], s_st)
    if "ctkv" in debug:
        t = dbg_out("ctkv", (4, 128, S))
        ctx.wait("gpsimd", s_sc)
        for f in range(4):
            ctx.dma("gpsimd", t[f], ctkv[f][:], s_st)
    if "ctq" in debug:
        t = dbg_out("ctq", (4, 128, NT))
        ctx.wait("gpsimd", s_sc)
        for f in range(4):
            ctx.dma("gpsimd", t[f], ctq[f][:], s_st)
    if "K" in debug:
        t = dbg_out("K", (8, 128, S))
        ctx.wait("gpsimd", s_ve)
        for p in range(8):
            ctx.dma("gpsimd", t[p], Kp[p][:], s_st)

    phA.close()

    QB = 512

    def phase_waits(engs):
        for e in engs:
            for s in (s_ve, s_sc, s_mm, s_st, s_rope):
                ctx.wait(e, s)

    phBC = ExitStack()
    attnout = [ctx.sbuf(f"ao{p}", [128, NT], BF16, stack=phBC, side="right")
               for p in range(8)]
    if max_phase >= 2:
        phB = ExitStack()
        phase_waits(("sync", "gpsimd"))
        Qall = [ctx.sbuf(f"Qp{p}", [128, NT], BF16, stack=phB, side="right")
                for p in range(8)]
        m1qb = ctx.sbuf("m1qb", [128, 256], BF16, stack=phB, side="right")
        cosq = ctx.sbuf("cosq", [128, 256], F32, stack=phB, side="right")
        sinq = ctx.sbuf("sinq", [128, 256], F32, stack=phB, side="right")
        ttB0 = ctx.sbuf("ttB0", [128, 256], F32, stack=phB, side="right")
        ttB1 = ctx.sbuf("ttB1", [128, 256], F32, stack=phB, side="right")

        phBw = ExitStack()
        w_uq = load_weight_bf16("a_wuq", inp["a_wuq"], DQ, D, stack=phBw, side="right")
        ctx.wait("tensor", s_ld)
        ctx.wait("tensor", s_ve)
        for p in range(8):
            for s8 in range(8):
                q0 = s8 * 256
                mm_group(B[s8 % 2], lambda k, _p=p: w_uq[:, k * D + _p * 128:k * D + _p * 128 + 128],
                         lambda k, _q=q0: ctq[k][:, _q:_q + 256], 4, 256)
                evac(B[s8 % 2], Qall[p][:, q0:q0 + 256], 256,
                     bias=smalls["a_buq"][:, p:p + 1])
        phBw.close()
        phBw = ExitStack()
        phase_waits(("sync",))
        w_qr = load_weight_bf16("a_wqr", inp["a_wqr"], DQ, D, stack=phBw, side="right")
        ctx.wait("tensor", s_ld)
        ctx.wait("tensor", s_ve)
        q_guards = [None]
        for p in range(8):
            for s8 in range(8):
                q0 = s8 * 256
                if q_guards[0] is not None:
                    ctx.wait("gpsimd", s_ve, q_guards[0])
                ctx.dma("gpsimd", cosq[:], inp["cosk"][p * 128:(p + 1) * 128, q0:q0 + 256], s_rope)
                ctx.dma("gpsimd", sinq[:], inp["sink"][p * 128:(p + 1) * 128, q0:q0 + 256], s_rope)
                qb0 = B[2 * (s8 % 2)]
                qb1 = B[2 * (s8 % 2) + 1]
                mm_group(qb0, lambda k, _p=p: w_qr[:, k * D + _p * 128:k * D + _p * 128 + 128],
                         lambda k, _q=q0: ctq[k][:, _q:_q + 256], 4, 256)
                if q_guards[0] is not None:
                    ctx.wait("scalar", s_ve, q_guards[0])
                evac(qb0, m1qb[:], 256, bias=smalls["a_bqr"][:, p:p + 1])
                qb1.mm_wait()
                ctx.wait("tensor", s_sc)
                ctx.op("tensor", "matmul", qb1.ap(128, 256), permb[:], m1qb[:],
                       start=True, stop=True, inc=[(s_mm, 1)])
                ctx.wait("vector", s_mm)
                ctx.wait("vector", s_rope)
                ctx.op("vector", "tensor_mul", ttB0[:], m1qb[:], cosq[:])
                ctx.op("vector", "tensor_mul", ttB1[:], qb1.ap(128, 256), sinq[:])
                ctx.op("vector", "tensor_add", ttB0[:], ttB0[:], ttB1[:])
                ctx.op("vector", "tensor_add", out=Qall[p][:, q0:q0 + 256],
                       in0=ttB0[:], in1=Qall[p][:, q0:q0 + 256], inc=[(s_ve, 1)])
                qb1.set_guard(s_ve)
                q_guards[0] = s_ve.n
        phBw.close()
        phQ.close()
        phase_waits(("sync",))
        w_uv = load_weight_bf16("a_wuv", inp["a_wuv"], DKV, D, stack=phB, side="right")
        buvr = load_small("a_buv", phB)
        identf = ctx.sbuf("identf", [128, 128], F32, stack=phB, side="right")
        ctx.dma("sync", identf[:], inp["ident"][:], s_ld)
        V2 = ctx.sbuf("V2", [128, 2 * NT], BF16, stack=phB, side="right")
        Ebuf = ctx.sbuf("Ebuf", [128, 3 * 1024], BF16, stack=phB, side="right")
        scrB = ctx.sbuf("scrB", [128, 512], F32, stack=phB, side="right")
        ava = ctx.sbuf("ava", [128, 512], F32, stack=phB, side="right")
        normc = ctx.sbuf("normc", [128, 34], F32, stack=phB, side="right")
        rowsB = ctx.sbuf("rowsB", [1, 1024], F32, stack=phB, side="right")
        cells = ctx.sbuf("cells", [1, 72], F32, stack=phB, side="right")
        ebias = ctx.sbuf("ebias", [128, 8], F32, stack=phB, side="right")

        ctx.wait("tensor", s_ld)
        SUBN = max_phase >= 3
        def norms_max(src_tile, ncc, cell_off, p):
            for j in (0, 1):
                nb = B[6]
                nb.mm_wait()
                cnt = 0
                for cc in range(ncc):
                    if cc > 0:
                        ctx.wait("scalar", s_mm)
                    ctx.op("scalar", "activation", scrB[:], src_tile[:, cc * 512:(cc + 1) * 512],
                           AF.Square, inc=[(s_sc, 1)])
                    ctx.wait("tensor", s_sc)
                    for c4 in range(4):
                        ctx.op("tensor", "matmul",
                               nb.t[0:128, nb.base + cnt:nb.base + cnt + 1],
                               scrB[j * 64:(j + 1) * 64, c4 * 128:(c4 + 1) * 128],
                               ones128[j * 64:(j + 1) * 64, :],
                               start=(cnt == 0), stop=(cnt == ncc * 4 - 1),
                               inc=([(s_mm, 1)] if c4 == 3 else None))
                        cnt += 1
                ctx.wait("scalar", s_mm)
                ctx.op("scalar", "copy", normc[:, 0:cnt], nb.t[0:128, nb.base:nb.base + cnt],
                       inc=[(s_sc, 1)])
                nb.set_guard(s_sc)
                ctx.wait("vector", s_sc)
                ctx.op("vector", "reduce_max", normc[:, 32:33], normc[:, 0:cnt], axis=AX.X,
                       inc=[(s_ve, 1)])
                B[7].mm_wait()
                ctx.wait("tensor", s_ve)
                ctx.op("tensor", "transpose", B[7].ap(1, 128), normc[:, 32:33], identf[:],
                       inc=[(s_mm, 1)])
                ctx.wait("scalar", s_mm)
                ctx.op("scalar", "copy", rowsB[:, 0:128], B[7].ap(1, 128), inc=[(s_sc, 1)])
                B[7].set_guard(s_sc)
                ctx.wait("vector", s_sc)
                ctx.op("vector", "reduce_max", cells[:, cell_off + 2 * p + j:cell_off + 2 * p + j + 1],
                       rowsB[:, 0:128], axis=AX.X, inc=[(s_ve, 1)])

        for p in range(8):
            if not SUBN:
                break
            norms_max(Kp[p], 8, 0, p)
            norms_max(Qall[p], 4, 16, p)

        for p in range(8 if SUBN else 0):
            c4 = 32 + 4 * p
            ctx.wait("vector", s_ve)
            ctx.op("vector", "tensor_max", cells[:, c4:c4 + 1], cells[:, 2 * p:2 * p + 1],
                   cells[:, 2 * p + 1:2 * p + 2])
            ctx.op("vector", "tensor_max", cells[:, c4 + 1:c4 + 2],
                   cells[:, 16 + 2 * p:16 + 2 * p + 1],
                   cells[:, 17 + 2 * p:18 + 2 * p], inc=[(s_ve, 1)])
            ctx.wait("scalar", s_ve)
            ctx.op("scalar", "mul", cells[:, c4 + 2:c4 + 3], cells[:, c4:c4 + 1],
                   cells[:, c4 + 1:c4 + 2])
            ctx.op("scalar", "activation", cells[:, c4 + 3:c4 + 4],
                   cells[:, c4 + 2:c4 + 3], AF.Sqrt, inc=[(s_sc, 1)])
            B[7].mm_wait()
            ctx.wait("tensor", s_sc)
            ctx.op("tensor", "matmul", B[7].ap(128, 1), ones_col[:],
                   cells[:, c4 + 3:c4 + 4],
                   start=True, stop=True, inc=[(s_mm, 1)])
            ctx.wait("scalar", s_mm)
            ctx.op("scalar", "activation", ebias[:, p:p + 1], B[7].ap(128, 1), AF.Identity,
                   scale=-1.0 / SCALE, inc=[(s_sc, 1)])
            B[7].set_guard(s_sc)

        if "enorm" in debug:
            t = dbg_out("enorm", (1, 40))
            t2 = dbg_out("ebias", (128, 8))
            ctx.wait("gpsimd", s_sc)
            ctx.wait("gpsimd", s_ve)
            ctx.dma("gpsimd", t[:], cells[:], s_st)
            ctx.dma("gpsimd", t2[:], ebias[:], s_st)
        NKC2 = S // 128
        ebuf_guard = [None] * 2
        for p in range(8 if max_phase >= 4 else 0):
            vb = B[7]
            for j in (0, 1):
                h = 2 * p + j
                for g8 in range(8):
                    vb.mm_wait()
                    for i4 in range(4):
                        kt = g8 * 4 + i4
                        for c in range(4):
                            ctx.op("tensor", "matmul", vb.ap(128, 64, off=i4 * 64),
                                   ctkv[c][:, kt * 128:(kt + 1) * 128],
                                   w_uv[:, c * D + h * 64:c * D + h * 64 + 64],
                                   start=(c == 0 and i4 == 0), stop=False)
                        ctx.op("tensor", "matmul", vb.ap(128, 64, off=i4 * 64),
                               ones_col[:], buvr[:, h * 64:h * 64 + 64],
                               start=False, stop=(i4 == 3),
                               inc=([(s_mm, 1)] if i4 == 3 else None))
                    ctx.wait("scalar", s_mm)
                    ctx.op("scalar", "copy", V2[:, j * NT + g8 * 256:j * NT + (g8 + 1) * 256],
                           vb.ap(128, 256), inc=[(s_sc, 1)])
                    vb.set_guard(s_sc)
            for qi in range(NT // QB if SUB != "v" else 0):
                q0 = qi * QB
                avb = B[6]
                avb.mm_wait()
                dn0 = B[7]
                dn0.mm_wait()
                B[4].mm_wait()
                av_started = False
                for g in range(NKC2):
                    kc = g
                    reg = REG[g % 2]
                    rb0, rb1 = B[2 * (g % 2)], B[2 * (g % 2) + 1]
                    rb0.mm_wait()
                    rb1.mm_wait()
                    for j in (0, 1):
                        ctx.op("tensor", "matmul",
                               reg[0:128, j * 512:(j + 1) * 512],
                               Kp[p][j * 64:(j + 1) * 64, kc * 128:(kc + 1) * 128],
                               Qall[p][j * 64:(j + 1) * 64, q0:q0 + QB],
                               start=True, stop=True,
                               inc=([(s_mm, 1)] if j == 1 else None))
                    eb = Ebuf[:, (g % 2) * 1024:(g % 2 + 1) * 1024]
                    ctx.wait("scalar", s_mm)
                    if ebuf_guard[g % 2] is not None:
                        ctx.wait("scalar", s_mm, ebuf_guard[g % 2])
                    ctx.op("scalar", "activation", eb, reg[0:128, 0:1024], AF.Exp,
                           bias=ebias[:, p:p + 1], scale=1.0 / SCALE, inc=[(s_sc, 1)])
                    rb0.set_guard(s_sc)
                    rb1.set_guard(s_sc)
                    last = (g == NKC2 - 1)
                    ctx.wait("tensor", s_sc)
                    if SUB not in ("s", "s0"):
                        for j in (0, 1):
                            esl = eb[:, j * 512:(j + 1) * 512]
                            ctx.op("tensor", "matmul", avb.ap(64, QB, p0=j * 64),
                                   V2[:, j * NT + kc * 64:j * NT + (kc + 1) * 64],
                                   esl, start=(not av_started),
                                   stop=last,
                                   tile_position=(0, 64 * j),
                                   skip_group_check=(j == 1))
                            dap = dn0.ap(1, QB) if j == 0 else B[4].ap(1, QB)
                            ctx.op("tensor", "matmul", dap,
                                   ones128b[:], esl,
                                   start=(not av_started),
                                   stop=last,
                                   inc=([(s_mm, 1)] if (last and j == 1) else None))
                        av_started = True
                    ebuf_guard[g % 2] = s_mm.n
                if SUB in ("s", "s0", "av"):
                    continue
                ctx.wait("vector", s_mm)
                ctx.op("vector", "reciprocal", rowsB[:, 0:QB], dn0.ap(1, QB))
                ctx.op("vector", "reciprocal", rowsB[:, QB:2 * QB], B[4].ap(1, QB),
                       inc=[(s_ve, 1)])
                dn0.set_guard(s_ve)
                B[4].set_guard(s_ve)
                ctx.wait("scalar", s_mm)
                ctx.op("scalar", "copy", ava[:], avb.ap(128, QB), inc=[(s_sc, 1)])
                avb.set_guard(s_sc)
                bcb = B[5]
                bcb.mm_wait()
                ctx.wait("tensor", s_ve)
                ctx.op("tensor", "matmul", bcb.ap(64, QB), ones_col[:, 0:64],
                       rowsB[:, 0:QB], start=True, stop=True, tile_position=(0, 0))
                ctx.op("tensor", "matmul", bcb.ap(64, QB, p0=64), ones_col[:, 0:64],
                       rowsB[:, QB:2 * QB], start=True, stop=True, tile_position=(0, 64),
                       skip_group_check=True, inc=[(s_mm, 1)])
                ctx.wait("vector", s_mm)
                ctx.wait("vector", s_sc)
                ctx.op("vector", "tensor_mul", attnout[p][:, q0:q0 + QB], ava[:],
                       bcb.ap(128, QB), inc=[(s_ve, 1)])
                bcb.set_guard(s_ve)

        if "attnout" in debug:
            t = dbg_out("attnout", (8, 128, NT))
            ctx.wait("gpsimd", s_ve)
            for p in range(8):
                ctx.dma("gpsimd", t[p], attnout[p][:], s_st)
        phB.close()
    phAB.close()

    r_t = [ctx.sbuf(f"r{f}", [128, NT], F32) for f in range(8)]
    if max_phase >= 5:
        phC = ExitStack()
        phase_waits(("sync", "gpsimd"))
        w_o = load_weight_bf16("a_wo", inp["a_wo"], D, D, stack=phC, side="right")
        xob = ctx.sbuf("xob", [128, 2 * 512], F32, stack=phC, side="right")
        scrC = [ctx.sbuf(f"scrC{i}", [128, 512], F32, stack=phC, side="right")
                for i in range(2)]
        xo_guard = [None, None]
        ctx.wait("tensor", s_ld)
        for oc in range(8):
            for s4 in range(4):
                q0 = s4 * 512
                pb = s4 % 2
                if xo_guard[pb] is not None:
                    ctx.wait("gpsimd", s_ve, xo_guard[pb])
                ctx.dma("gpsimd", xob[:, pb * 512:(pb + 1) * 512],
                        inp["x_own"][oc * 128:(oc + 1) * 128, q0:q0 + 512], s_rope)
                ctx.wait("tensor", s_ve)
                mm_group(B[oc % 2], lambda k, _o=oc: w_o[:, k * D + _o * 128:k * D + _o * 128 + 128],
                         lambda k, _q=q0: attnout[k][:, _q:_q + 512], 8, 512)
                ctx.wait("scalar", s_mm)
                ctx.wait("scalar", s_ve)
                ctx.op("scalar", "activation", scrC[pb][:], B[oc % 2].ap(128, 512), AF.Identity,
                       bias=smalls["a_bo"][:, oc:oc + 1], inc=[(s_sc, 1)])
                B[oc % 2].set_guard(s_sc)
                ctx.wait("vector", s_sc)
                ctx.wait("vector", s_rope)
                ctx.op("vector", "tensor_add", r_t[oc][:, q0:q0 + 512], scrC[pb][:],
                       xob[:, pb * 512:(pb + 1) * 512], inc=[(s_ve, 1)])
                xo_guard[pb] = s_ve.n
        phC.close()
        phBC.close()

    if "r1" in debug:
        t = dbg_out("r1", (8, 128, NT))
        ctx.wait("gpsimd", s_ve)
        for f in range(8):
            ctx.dma("gpsimd", t[f], r_t[f][:], s_st)

    ln_uid = [0]

    def make_ln_scratch(stack):
        ln_uid[0] += 1
        u = ln_uid[0]
        sc = {}
        for nm, shape in (("sq", [128, 512]), ("mu_b", [128, 512]),
                          ("rs_b", [128, 512]), ("t0", [128, 512]),
                          ("murow", [1, 512]), ("rsrow", [1, 512]),
                          ("va", [1, 512]), ("t1", [1, 512]),
                          ("mu2", [1, 512])):
            sc[nm] = ctx.sbuf(f"ln{u}_{nm}", shape, F32, stack=stack, side="right")
        return sc

    def emit_ln(srcs, T, outs, sc, sA, sB, bC, gamma=None, beta=None):
        mu_row = sc["murow"][:, 0:T]
        rs_row = sc["rsrow"][:, 0:T]
        va_row = sc["va"][:, 0:T]
        t1_row = sc["t1"][:, 0:T]
        mu2_row = sc["mu2"][:, 0:T]
        ctx.wait("tensor", s_ve)
        ctx.wait("tensor", s_sc)
        ctx.wait("scalar", s_ve)
        sA.mm_wait()
        for f in range(8):
            ctx.op("tensor", "matmul", sA.ap(1, T), ones128[:], srcs[f],
                   start=(f == 0), stop=(f == 7), inc=([(s_mm, 1)] if f == 7 else None))
        sB.mm_wait()
        for f in range(8):
            if sq_guard:
                ctx.wait("scalar", s_mm, val=sq_guard[-1])
            ctx.op("scalar", "activation", sc["sq"][:, 0:T], srcs[f], AF.Square,
                   inc=[(s_sc, 1)])
            ctx.wait("tensor", s_sc)
            ctx.op("tensor", "matmul", sB.ap(1, T), ones128[:], sc["sq"][:, 0:T],
                   start=(f == 0), stop=(f == 7), inc=[(s_mm, 1)])
            sq_guard.append(s_mm.n)
        ctx.wait("vector", s_mm)
        ctx.op("vector", "tensor_scalar_mul", mu_row, sA.ap(1, T), 1.0 / D)
        ctx.op("vector", "tensor_scalar_mul", t1_row, sB.ap(1, T), 1.0 / D,
               inc=[(s_ve, 1)])
        sA.set_guard(s_ve)
        sB.set_guard(s_ve)
        ctx.op("vector", "tensor_mul", va_row, mu_row, mu_row)
        ctx.op("vector", "tensor_sub", va_row, t1_row, va_row)
        ctx.op("vector", "tensor_scalar_add", va_row, va_row, EPS, inc=[(s_ve, 1)])
        ctx.wait("scalar", s_ve)
        ctx.op("scalar", "activation", rs_row, va_row, AF.Sqrt, inc=[(s_sc, 1)])
        ctx.wait("vector", s_sc)
        ctx.op("vector", "reciprocal", t1_row, rs_row, inc=[(s_ve, 1)])
        ctx.wait("scalar", s_ve)
        ctx.op("scalar", "square", mu2_row, t1_row, inc=[(s_sc, 1)])
        ctx.wait("vector", s_sc)
        ctx.op("vector", "tensor_mul", mu2_row, mu2_row, va_row)
        ctx.op("vector", "tensor_scalar", mu2_row, mu2_row, -0.5, 1.5,
               op0=ALU.mult, op1=ALU.add)
        ctx.op("vector", "tensor_mul", rs_row, mu2_row, t1_row, inc=[(s_ve, 1)])
        bC.mm_wait()
        sA.mm_wait()
        ctx.wait("tensor", s_ve)
        ctx.op("tensor", "matmul", bC.ap(128, T, 0), ones_col[:], mu_row,
               start=True, stop=True)
        ctx.op("tensor", "matmul", sA.ap(128, T, 0), ones_col[:], rs_row,
               start=True, stop=True, inc=[(s_mm, 1)])
        ctx.wait("scalar", s_mm)
        ctx.wait("scalar", s_ve)
        ctx.op("scalar", "copy", sc["mu_b"][:, 0:T], bC.ap(128, T, 0))
        ctx.op("scalar", "copy", sc["rs_b"][:, 0:T], sA.ap(128, T, 0), inc=[(s_sc, 1)])
        bC.set_guard(s_sc)
        sA.set_guard(s_sc)
        ctx.wait("vector", s_sc)
        for f in range(8):
            ctx.op("vector", "tensor_sub", sc["t0"][:, 0:T], srcs[f], sc["mu_b"][:, 0:T])
            if gamma is None:
                ctx.op("vector", "tensor_mul", outs[f], sc["t0"][:, 0:T],
                       sc["rs_b"][:, 0:T], inc=[(s_ve, 1)])
            else:
                ctx.op("vector", "tensor_mul", sc["t0"][:, 0:T], sc["t0"][:, 0:T],
                       sc["rs_b"][:, 0:T])
                ctx.op("vector", "tensor_scalar", outs[f], sc["t0"][:, 0:T],
                       gamma[:, f:f + 1], beta[:, f:f + 1],
                       op0=ALU.mult, op1=ALU.add, inc=[(s_ve, 1)])

    class WStream:

        _uid = [0]

        def __init__(self, dram, ncols, stack, stg_cols=None, nbuf=2):
            WStream._uid[0] += 1
            u = WStream._uid[0]
            self.dram = dram
            self.ncols = ncols
            self.nbuf = nbuf
            self.stg_cols = min(ncols, stg_cols or ncols)
            self.piece = min(2048, self.stg_cols)
            self.nslots = max(1, self.stg_cols // self.piece)
            self.stg = ctx.sbuf(f"wstg{u}", [128, self.stg_cols], F32,
                                stack=stack, side="right")
            self.wb = [ctx.sbuf(f"wb{u}_{i}", [128, ncols], BF16, stack=stack, side="right")
                       for i in range(nbuf)]
            self.stg_guard = [None] * max(1, self.nslots)
            self.wb_guard = [None] * nbuf
            self.ready = [0] * nbuf
            self.born = False

        def load(self, oc, buf, width=None):
            if not self.born:
                for s in (s_mm, s_ve, s_sc):
                    ctx.wait("sync", s)
                self.born = True
            width = self.ncols if width is None else width
            for i, c0 in enumerate(range(0, width, self.piece)):
                cw = min(self.piece, width - c0)
                sl = (i % self.nslots) * self.piece
                sw = min(self.piece, self.stg_cols - sl)
                assert cw <= sw
                if self.stg_guard[i % self.nslots] is not None:
                    ctx.wait("sync", s_ve, self.stg_guard[i % self.nslots])
                ctx.dma("sync", self.stg[:, sl:sl + cw], self.dram[oc][:, c0:c0 + cw],
                        s_ld)
                if self.wb_guard[buf] is not None:
                    ctx.wait("vector", s_mm, self.wb_guard[buf])
                ctx.wait("vector", s_ld)
                ctx.op("vector", "tensor_copy", out=self.wb[buf][:, c0:c0 + cw],
                       in_=self.stg[:, sl:sl + cw], inc=[(s_ve, 1)])
                self.stg_guard[i % self.nslots] = s_ve.n
            self.ready[buf] = s_ve.n

        def used(self, buf):
            self.wb_guard[buf] = s_mm.n

    TQ = 512
    if max_phase >= 6:
        phD = ExitStack()
        phase_waits(("sync", "gpsimd", "scalar"))
        lnsc = make_ln_scratch(phD)
        xq = ctx.sbuf("xqD", [128, 8 * TQ], BF16, stack=phD, side="right")
        xq_t = [xq[:, f * TQ:(f + 1) * TQ] for f in range(8)]
        hq = ctx.sbuf("hqD", [128, 32 * TQ], BF16, stack=phD, side="right")
        g1q = ctx.sbuf("g1qD", [128, 32 * TQ], BF16, stack=phD, side="right")
        tD0 = lnsc["mu_b"]
        tD1 = lnsc["rs_b"]
        wsh = WStream(inp["f_h1x"], DH, phD)
        ws12 = wsh

        for q in range(NT // TQ):
            q0 = q * TQ
            emit_ln([r_t[f][:, q0:q0 + TQ] for f in range(8)], TQ, xq_t,
                    lnsc, B[4], B[5], B[3])
            ws12.dram = inp["f_w12x"]
            ws12.load(0, 0, width=2048)
            ctx.wait("tensor", s_ve)
            ctx.wait("tensor", s_sc)
            for oc in range(32):
                bu = oc % 2
                if oc < 31:
                    ws12.load(oc + 1, 1 - bu, width=2048)
                wb = ws12.wb[bu]
                ctx.wait("tensor", s_ve, ws12.ready[bu])
                mm_group(B[2 * bu], lambda k, _w=wb: _w[:, k * 128:(k + 1) * 128],
                         lambda k: xq_t[k], 8, TQ)
                mm_group(B[2 * bu + 1], lambda k, _w=wb: _w[:, 1024 + k * 128:1024 + (k + 1) * 128],
                         lambda k: xq_t[k], 8, TQ)
                ws12.used(bu)
                ctx.wait("scalar", s_mm)
                ctx.wait("scalar", s_ve)
                ctx.op("scalar", "activation", tD0[:], B[2 * bu].ap(128, TQ), AF.Silu,
                       scale=smalls["f_s1"][:, oc:oc + 1])
                ctx.op("scalar", "activation", tD1[:], B[2 * bu + 1].ap(128, TQ),
                       AF.Identity, scale=smalls["f_s2"][:, oc:oc + 1], inc=[(s_sc, 1)])
                B[2 * bu].set_guard(s_sc)
                B[2 * bu + 1].set_guard(s_sc)
                ctx.wait("vector", s_sc)
                ctx.op("vector", "tensor_mul", hq[:, oc * TQ:(oc + 1) * TQ],
                       tD0[:], tD1[:], inc=[(s_ve, 1)])
            for stage, dram, scl, src_q, dst in (
                    (2, inp["f_h1x"], "f_sh1", hq, g1q),
                    (3, inp["f_h2x"], "f_sh2", g1q, hq)):
                wsh.dram = dram
                wsh.load(0, 0)
                ctx.wait("tensor", s_ve)
                ctx.wait("tensor", s_sc)
                for oc in range(32):
                    bu = oc % 2
                    if oc < 31:
                        wsh.load(oc + 1, 1 - bu)
                    wb = wsh.wb[bu]
                    ctx.wait("tensor", s_ve, wsh.ready[bu])
                    mm_group(B[2 * bu], lambda k, _w=wb: _w[:, k * 128:(k + 1) * 128],
                             lambda k, _s=src_q: _s[:, k * TQ:k * TQ + TQ], 32, TQ)
                    wsh.used(bu)
                    ctx.wait("scalar", s_mm)
                    ctx.op("scalar", "activation", dst[:, oc * TQ:(oc + 1) * TQ],
                           B[2 * bu].ap(128, TQ), AF.Gelu,
                           scale=smalls[scl][:, oc:oc + 1], inc=[(s_sc, 1)])
                    B[2 * bu].set_guard(s_sc)
            wsh.dram = inp["f_w3x"]
            wsh.load(0, 0)
            ctx.wait("tensor", s_sc)
            for oc in range(8):
                bu = oc % 2
                if oc < 7:
                    wsh.load(oc + 1, 1 - bu)
                wb = wsh.wb[bu]
                ctx.wait("tensor", s_ve, wsh.ready[bu])
                mm_group(B[2 * bu], lambda k, _w=wb: _w[:, k * 128:(k + 1) * 128],
                         lambda k: hq[:, k * TQ:k * TQ + TQ], 32, TQ)
                wsh.used(bu)
                ctx.wait("scalar", s_mm)
                ctx.wait("scalar", s_ve)
                ctx.op("scalar", "activation", tD0[:] if bu == 0 else tD1[:],
                       B[2 * bu].ap(128, TQ), AF.Identity,
                       scale=smalls["f_s3"][:, oc:oc + 1], inc=[(s_sc, 1)])
                B[2 * bu].set_guard(s_sc)
                ctx.wait("vector", s_sc)
                ctx.op("vector", "tensor_add", r_t[oc][:, q0:q0 + TQ],
                       tD0[:] if bu == 0 else tD1[:], r_t[oc][:, q0:q0 + TQ],
                       inc=[(s_ve, 1)])
        phD.close()

    if "r2" in debug:
        t = dbg_out("r2", (8, 128, NT))
        ctx.wait("gpsimd", s_ve)
        for f in range(8):
            ctx.dma("gpsimd", t[f], r_t[f][:], s_st)

    s_cc = ctx.sem("s_cc")
    cc_uid = [0]
    TH = NT // 2

    def allgather_prev(src_tile, kcols, recv_tile, stack):
        nc = ctx.nc
        cc_uid[0] += 1
        u = cc_uid[0]
        ncols = 128 * kcols
        cin = nc.dram_tensor(f"ccin{u}", [1, ncols], F32).ap()
        cout = nc.dram_tensor(f"ccout{u}", [8, ncols], F32).ap()
        ags = ctx.sbuf(f"ags{u}", [8, ncols], F32, stack=stack, side="right")
        rowt = ctx.sbuf(f"ccrow{u}", [1, ncols], F32, stack=stack, side="right")
        ctx.wait("gpsimd", s_ve)
        ctx.wait("gpsimd", s_sc)
        ctx.dma("gpsimd", cin.rearrange("o (p c) -> (o p) c", p=128), src_tile, s_st)
        ctx.wait("gpsimd", s_st)
        ctx.op("gpsimd", "collective_compute", "AllGather", ALU.bypass,
               replica_groups=[list(range(8))], ins=[cin[:]], outs=[cout[:]],
               inc=[(s_cc, 1)])
        ctx.wait("gpsimd", s_cc)
        ctx.dma("gpsimd", ags[:], cout[:], s_gd)
        ctx.wait("tensor", s_gd)
        ctx.wait("tensor", s_ld)
        rb = B[0]
        rb.mm_wait()
        B[1].mm_wait()
        npieces = (ncols + 1023) // 1024
        for pz in range(npieces):
            p0 = pz * 1024
            pw = min(1024, ncols - p0)
            for i in range(0, pw, 512):
                cw = min(512, pw - i)
                ctx.op("tensor", "matmul", REG[0][0:1, i:i + cw],
                       smalls["mask8"][:], ags[:, p0 + i:p0 + i + cw],
                       start=True, stop=True,
                       inc=([(s_mm, 1)] if i + cw == pw else None))
            ctx.wait("scalar", s_mm)
            ctx.op("scalar", "copy", rowt[:, p0:p0 + pw], REG[0][0:1, 0:pw],
                   inc=[(s_sc, 1)])
            rb.set_guard(s_sc)
            B[1].set_guard(s_sc)
        ctx.wait("gpsimd", s_sc)
        ctx.dma("gpsimd", recv_tile, rowt[:], s_gd)
        return s_gd.n

    if max_phase >= 7:
        phE = ExitStack()
        phase_waits(("sync", "gpsimd", "scalar"))
        lnsc = make_ln_scratch(phE)
        NQ = 4
        TQE = 512
        wrow = ctx.sbuf("wrow", [1, D], F32, stack=phE, side="right")
        drow = load_small("r_decay_row", phE)
        ccol = ctx.sbuf("ccolE", [128, 32], F32, stack=phE, side="right")
        w_col = ccol[:, 0:8]
        ew_col = ccol[:, 8:16]
        eu_col = ccol[:, 16:24]
        ewq_col = ccol[:, 24:32]
        ctx.wait("scalar", s_ld)
        ctx.op("scalar", "activation", wrow[:], drow[:], AF.Exp)
        ctx.op("scalar", "activation", wrow[:], wrow[:], AF.Identity, scale=-1.0)
        ctx.op("scalar", "activation", w_col, smalls["r_decay"][:], AF.Exp)
        ctx.op("scalar", "activation", w_col, w_col, AF.Identity, scale=-1.0)
        ctx.op("scalar", "activation", ew_col, w_col, AF.Exp)
        ctx.op("scalar", "activation", eu_col, smalls["r_first"][:], AF.Exp)
        ctx.op("scalar", "activation", ewq_col, w_col, AF.Exp, scale=float(TQE),
               inc=[(s_sc, 1)])
        iotar = load_small("iota_rev", phE)
        recvh = ctx.sbuf("recvh", [128, 8], F32, stack=phE, side="right")
        recvAB = ctx.sbuf("recvAB", [128, 16], F32, stack=phE, side="right")
        hlast_c = ctx.sbuf("hlast_c", [128, 8], F32, stack=phE, side="right")
        dots = ctx.sbuf("dotsE", [128, 64], F32, stack=phE, side="right")
        dA = [dots[:, q * 16:q * 16 + 8] for q in range(4)]
        dB = [dots[:, q * 16 + 8:q * 16 + 16] for q in range(4)]
        sendAB = ctx.sbuf("sendAB", [128, 16], F32, stack=phE, side="right")
        ABfin = ctx.sbuf("ABfin", [128, 16], F32, stack=phE, side="right")
        Afin = ABfin[:, 0:8]
        Bfin = ABfin[:, 8:16]
        g_last = ctx.sbuf("g_lastE", [128, 8], F32, stack=phE, side="right")
        g0b = [ctx.sbuf(f"g0b{f}", [128, 513], BF16, stack=phE, side="right")
               for f in range(8)]
        ccscr = ctx.sbuf("ccscr", [8, 2048], F32, stack=phE, side="right")

        def allgather_prev(src_tile, kcols, recv_tile):
            nc = ctx.nc
            cc_uid[0] += 1
            u = cc_uid[0]
            ncols = 128 * kcols
            cin = nc.dram_tensor(f"ccin{u}", [1, ncols], F32).ap()
            cout = nc.dram_tensor(f"ccout{u}", [8, ncols], F32).ap()
            ctx.wait("gpsimd", s_ve)
            ctx.wait("gpsimd", s_sc)
            ctx.dma("gpsimd", cin.rearrange("o (p c) -> (o p) c", p=128), src_tile, s_st)
            ctx.wait("gpsimd", s_st)
            ctx.op("gpsimd", "collective_compute", "AllGather", ALU.bypass,
                   replica_groups=[list(range(8))], ins=[cin[:]], outs=[cout[:]],
                   inc=[(s_cc, 1)])
            ctx.wait("gpsimd", s_cc)
            ctx.wait("gpsimd", s_sc)
            ctx.dma("gpsimd", ccscr[:, 0:ncols], cout[:], s_gd)
            if u == 1 and "eccscr" in debug:
                tdbg = dbg_out("eccscr", (8, 2048))
                ctx.wait("gpsimd", s_gd)
                ctx.dma("gpsimd", tdbg[:, 0:ncols], ccscr[:, 0:ncols], s_st)
            ctx.wait("tensor", s_gd)
            ctx.wait("tensor", s_ld)
            rb = B[0]
            for pz in range((ncols + 1023) // 1024):
                rb.mm_wait()
                B[1].mm_wait()
                p0 = pz * 1024
                pw = min(1024, ncols - p0)
                for i in range(0, pw, 512):
                    cw = min(512, pw - i)
                    ctx.op("tensor", "matmul", REG[0][0:1, i:i + cw],
                           smalls["mask8"][:], ccscr[0:8, p0 + i:p0 + i + cw],
                           start=True, stop=True,
                           inc=([(s_mm, 1)] if i + cw == pw else None))
                ctx.wait("scalar", s_mm)
                ctx.op("scalar", "copy", ccscr[0:1, p0:p0 + pw], REG[0][0:1, 0:pw],
                       inc=[(s_sc, 1)])
                rb.set_guard(s_sc)
                B[1].set_guard(s_sc)
            ctx.wait("gpsimd", s_sc)
            ctx.dma("gpsimd", recv_tile, ccscr[0:1, 0:ncols], s_gd)
            return s_gd.n

        def emit_h_quarter(q, tag, hb, h0t, want_xr, xkb, xvb, xrb):
            c0 = q * TQE
            emit_ln([r_t[f][:, c0:c0 + TQE] for f in range(8)], TQE,
                    [h0t[:, f * TQE:(f + 1) * TQE] for f in range(8)],
                    lnsc, B[4], B[5], B[3])
            emit_ln([h0t[:, f * TQE:(f + 1) * TQE] for f in range(8)], TQE,
                    [hb[f][:, 1:1 + TQE] for f in range(8)],
                    lnsc, B[4], B[5], B[3],
                    gamma=smalls["r_ln1w"], beta=smalls["r_ln1b"])
            if q == 0:
                ctx.wait("vector", s_gd, c1_done[0])
            for f in range(8):
                src = recvh[:, f:f + 1] if q == 0 else hlast_c[:, f:f + 1]
                ctx.op("vector", "tensor_copy", out=hb[f][:, 0:1], in_=src)
            for f in range(8):
                ctx.op("vector", "tensor_copy", out=hlast_c[:, f:f + 1],
                       in_=hb[f][:, TQE:TQE + 1])
            mixes = [(xkb, "r_mixk"), (xvb, "r_mixv")] +                     ([(xrb, "r_mixr")] if want_xr else [])
            for f in range(8):
                ctx.op("vector", "tensor_sub", lnsc["t0"][:], hb[f][:, 1:513],
                       hb[f][:, 0:512])
                for i, (dst, mixn) in enumerate(mixes):
                    ctx.op("vector", "scalar_tensor_tensor", out=dst[f][:],
                           in0=lnsc["t0"][:], scalar=smalls[mixn][:, f:f + 1],
                           in1=hb[f][:, 0:512], op0=ALU.mult, op1=ALU.add,
                           inc=([(s_ve, 1)] if (f == 7 and i == len(mixes) - 1) else None))

        def emit_ffn_slice(c0, gbuf, stack):
            gkb = [ctx.sbuf(f"gkbF{c0}_{f}", [128, 512], BF16, stack=stack, side="right")
                   for f in range(8)]
            grb = [ctx.sbuf(f"grbF{c0}_{f}", [128, 512], BF16, stack=stack, side="right")
                   for f in range(8)]
            kkb = ctx.sbuf(f"kkbF{c0}", [128, 32 * 512], BF16, stack=stack, side="right")
            ctx.wait("vector", s_ve)
            for f in range(8):
                ctx.op("vector", "tensor_sub", lnsc["t0"][:, 0:512],
                       gbuf[f][:, 1:513], gbuf[f][:, 0:512])
                ctx.op("vector", "scalar_tensor_tensor", out=gkb[f][:],
                       in0=lnsc["t0"][:, 0:512], scalar=smalls["r_fmixk"][:, f:f + 1],
                       in1=gbuf[f][:, 0:512], op0=ALU.mult, op1=ALU.add)
                ctx.op("vector", "scalar_tensor_tensor", out=grb[f][:],
                       in0=lnsc["t0"][:, 0:512], scalar=smalls["r_fmixr"][:, f:f + 1],
                       in1=gbuf[f][:, 0:512], op0=ALU.mult, op1=ALU.add,
                       inc=([(s_ve, 1)] if f == 7 else None))
            gk_rdy = s_ve.n
            wsK = WStream(inp["r_fwkx"], D, stack)
            wsK.load(0, 0)
            for oc in range(32):
                bu = oc % 2
                if oc < 31:
                    wsK.load(oc + 1, 1 - bu)
                wb = wsK.wb[bu]
                ctx.wait("tensor", s_ve, wsK.ready[bu])
                ctx.wait("tensor", s_ve, gk_rdy)
                mm_group(B[2 * bu], lambda k, _w=wb: _w[:, k * 128:(k + 1) * 128],
                         lambda k: gkb[k][:], 8, 512)
                wsK.used(bu)
                ctx.wait("scalar", s_mm)
                ctx.wait("scalar", s_ve)
                tdst = lnsc["mu_b"][:, 0:512] if bu == 0 else lnsc["rs_b"][:, 0:512]
                ctx.op("scalar", "activation", tdst, B[2 * bu].ap(128, 512), AF.Relu,
                       inc=[(s_sc, 1)])
                B[2 * bu].set_guard(s_sc)
                ctx.wait("vector", s_sc)
                ctx.op("vector", "tensor_mul", kkb[:, oc * 512:(oc + 1) * 512],
                       tdst, tdst, inc=[(s_ve, 1)])
            kk_rdy = s_ve.n
            wsV = WStream(inp["r_fwvrx"], DH + D, stack, stg_cols=2048, nbuf=1)
            for f in range(8):
                bu = 0
                wsV.load(f, 0)
                wb = wsV.wb[bu]
                ctx.wait("tensor", s_ve, wsV.ready[bu])
                ctx.wait("tensor", s_ve, kk_rdy)
                mm_group(B[0], lambda k, _w=wb: _w[:, k * 128:(k + 1) * 128],
                         lambda k: kkb[:, k * 512:(k + 1) * 512], 32, 512)
                mm_group(B[1], lambda k, _w=wb: _w[:, DH + k * 128:DH + (k + 1) * 128],
                         lambda k: grb[k][:], 8, 512)
                wsV.used(bu)
                ctx.wait("scalar", s_mm)
                ctx.wait("scalar", s_ve)
                tdst = lnsc["mu_b"][:, 0:512]
                ctx.op("scalar", "activation", tdst, B[1].ap(128, 512),
                       AF.Sigmoid, inc=[(s_sc, 1)])
                B[1].set_guard(s_sc)
                ctx.wait("vector", s_sc)
                ctx.op("vector", "tensor_mul", lnsc["t0"][:, 0:512],
                       B[0].ap(128, 512), tdst)
                ctx.op("vector", "tensor_add", r_t[f][:, c0:c0 + 512],
                       lnsc["t0"][:, 0:512], r_t[f][:, c0:c0 + 512], inc=[(s_ve, 1)])
                B[0].set_guard(s_ve)

        hce = ctx.sbuf("hce", [128, 8], F32, stack=phE, side="right")
        phE0 = ExitStack()
        tmp0 = ctx.sbuf("e0tmp0", [128, 8 * 512], F32, stack=phE0, side="right")
        tmph = [ctx.sbuf(f"e0h{f}", [128, 512], F32, stack=phE0, side="right")
                for f in range(8)]
        emit_ln([r_t[f][:, NT - 512:NT] for f in range(8)], 512,
                [tmp0[:, f * 512:(f + 1) * 512] for f in range(8)],
                lnsc, B[4], B[5], B[3])
        emit_ln([tmp0[:, f * 512:(f + 1) * 512] for f in range(8)], 512,
                [tmph[f][:] for f in range(8)], lnsc, B[4], B[5], B[3],
                gamma=smalls["r_ln1w"], beta=smalls["r_ln1b"])
        for f in range(8):
            ctx.op("vector", "tensor_copy", out=hce[:, f:f + 1],
                   in_=tmph[f][:, 511:512], inc=([(s_ve, 1)] if f == 7 else None))
        phE0.close()
        c1_done = [allgather_prev(hce[:], 8, recvh[:])]
        if "erecvh" in debug:
            t = dbg_out("erecvh", (128, 24))
            ctx.wait("gpsimd", s_gd)
            ctx.wait("gpsimd", s_ve)
            ctx.dma("gpsimd", t[:, 0:8], recvh[:], s_st)
            ctx.dma("gpsimd", t[:, 8:16], hce[:], s_st)
            ctx.dma("gpsimd", t[:, 16:24], hce[:], s_st)

        run_beta = SUBE in ("beta", "gamma", "full")
        phB2 = ExitStack()
        wsKVR = WStream(inp["r_wkvrx"], 3 * D, phB2, stg_cols=2048)
        for q in range(NQ if run_beta else 0):
            phQ2 = ExitStack()
            xkb = [ctx.sbuf(f"xkbB{q}_{f}", [128, TQE], BF16, stack=phQ2, side="right")
                   for f in range(8)]
            xvb = [ctx.sbuf(f"xvbB{q}_{f}", [128, TQE], BF16, stack=phQ2, side="right")
                   for f in range(8)]
            phQh = ExitStack()
            h0t = ctx.sbuf(f"h0tB{q}", [128, 8 * TQE], F32, stack=phQh, side="right")
            hb = [ctx.sbuf(f"hbB{q}_{f}", [128, TQE + 1], BF16, stack=phQh, side="right")
                  for f in range(8)]
            emit_h_quarter(q, "B", hb, h0t, False, xkb, xvb, None)
            mix_rdy = s_ve.n
            phQh.close()
            phQs = ExitStack()
            ekE = ctx.sbuf(f"ekB{q}", [128, TQE], F32, stack=phQs, side="right")
            ekvE = ctx.sbuf(f"ekvB{q}", [128, TQE], F32, stack=phQs, side="right")
            ewrev = ctx.sbuf(f"ewrevB{q}", [128, TQE], F32, stack=phQs, side="right")
            ttE = ctx.sbuf(f"ttBq{q}", [128, TQE], F32, stack=phQs, side="right")
            wsKVR.load(0, 0)
            for f in range(8):
                bu = f % 2
                if f < 7:
                    wsKVR.load(f + 1, 1 - bu)
                wb = wsKVR.wb[bu]
                ctx.wait("tensor", s_ve, wsKVR.ready[bu])
                ctx.wait("tensor", s_ve, mix_rdy)
                B[4].mm_wait()
                ctx.wait("tensor", s_sc)
                ctx.wait("tensor", s_ld)
                ctx.op("tensor", "matmul", B[4].ap(128, TQE),
                       wrow[:, f * 128:(f + 1) * 128], iotar[:, 0:TQE],
                       start=True, stop=True, inc=[(s_mm, 1)])
                ctx.wait("scalar", s_mm)
                ctx.op("scalar", "activation", ewrev[:], B[4].ap(128, TQE), AF.Exp,
                       inc=[(s_sc, 1)])
                B[4].set_guard(s_sc)
                mm_group(B[0], lambda k, _w=wb: _w[:, k * 128:(k + 1) * 128],
                         lambda k: xkb[k][:], 8, TQE)
                mm_group(B[2], lambda k, _w=wb: _w[:, D + k * 128:D + (k + 1) * 128],
                         lambda k: xvb[k][:], 8, TQE)
                ctx.wait("scalar", s_mm)
                ctx.op("scalar", "activation", ekE[:], B[0].ap(128, TQE), AF.Exp,
                       inc=[(s_sc, 1)])
                B[0].set_guard(s_sc)
                ctx.wait("vector", s_sc)
                ctx.op("vector", "tensor_mul", ekvE[:], B[2].ap(128, TQE), ekE[:],
                       inc=[(s_ve, 1)])
                B[2].set_guard(s_ve)
                ctx.op("vector", "tensor_mul", ttE[:], ekE[:], ewrev[:])
                ctx.op("vector", "reduce_sum", dB[q][:, f:f + 1], ttE[:], axis=AX.X)
                ctx.op("vector", "tensor_mul", ttE[:], ekvE[:], ewrev[:])
                ctx.op("vector", "reduce_sum", dA[q][:, f:f + 1], ttE[:], axis=AX.X,
                       inc=[(s_ve, 1)])
            phQs.close()
            phQ2.close()
        phB2.close()
        ctx.wait("vector", s_ve)
        for st, ds in (((0, dA), (8, dB)) if run_beta else ()):
            for f in range(8):
                ctx.op("vector", "scalar_tensor_tensor", out=sendAB[:, st + f:st + f + 1],
                       in0=ds[0][:, f:f + 1], scalar=ewq_col[:, f:f + 1],
                       in1=ds[1][:, f:f + 1], op0=ALU.mult, op1=ALU.add)
                ctx.op("vector", "scalar_tensor_tensor", out=sendAB[:, st + f:st + f + 1],
                       in0=sendAB[:, st + f:st + f + 1], scalar=ewq_col[:, f:f + 1],
                       in1=ds[2][:, f:f + 1], op0=ALU.mult, op1=ALU.add)
                ctx.op("vector", "scalar_tensor_tensor", out=sendAB[:, st + f:st + f + 1],
                       in0=sendAB[:, st + f:st + f + 1], scalar=ewq_col[:, f:f + 1],
                       in1=ds[3][:, f:f + 1], op0=ALU.mult, op1=ALU.add,
                       inc=([(s_ve, 1)] if f == 7 else None))
        c2_done = allgather_prev(sendAB[:], 16, recvAB[:]) if run_beta else 0
        if "edots" in debug:
            t = dbg_out("edots", (128, 96))
            ctx.wait("gpsimd", s_ve)
            ctx.wait("gpsimd", s_gd)
            ctx.dma("gpsimd", t[:, 0:64], dots[:], s_st)
            ctx.dma("gpsimd", t[:, 64:80], sendAB[:], s_st)
            ctx.dma("gpsimd", t[:, 80:96], recvAB[:], s_st)

        run_gamma = SUBE in ("gamma", "full")
        for q in range(NQ if run_gamma else 0):
            c0 = q * TQE
            phQ = ExitStack()
            rwkvb = [ctx.sbuf(f"rwkvG{q}_{f}", [128, TQE], BF16, stack=phQ, side="right")
                     for f in range(8)]
            phQ2 = ExitStack()
            xkb = [ctx.sbuf(f"xkbG{q}_{f}", [128, TQE], BF16, stack=phQ2, side="right")
                   for f in range(8)]
            xvb = [ctx.sbuf(f"xvbG{q}_{f}", [128, TQE], BF16, stack=phQ2, side="right")
                   for f in range(8)]
            xrb = [ctx.sbuf(f"xrbG{q}_{f}", [128, TQE], BF16, stack=phQ2, side="right")
                   for f in range(8)]
            phQh = ExitStack()
            h0t = ctx.sbuf(f"h0tG{q}", [128, 8 * TQE], F32, stack=phQh, side="right")
            hb = [ctx.sbuf(f"hbG{q}_{f}", [128, TQE + 1], BF16, stack=phQh, side="right")
                  for f in range(8)]
            emit_h_quarter(q, "G", hb, h0t, True, xkb, xvb, xrb)
            mix_rdy = s_ve.n
            phQh.close()
            phQs = ExitStack()
            wsKVR = WStream(inp["r_wkvrx"], 3 * D, phQs, stg_cols=2048)
            ekE = ctx.sbuf(f"ekG{q}", [128, TQE], F32, stack=phQs, side="right")
            ekvE = ctx.sbuf(f"ekvG{q}", [128, TQE], F32, stack=phQs, side="right")
            eukE = ctx.sbuf(f"eukG{q}", [128, TQE], F32, stack=phQs, side="right")
            evtE = ctx.sbuf(f"evtG{q}", [128, TQE], F32, stack=phQs, side="right")
            ewbE = ctx.sbuf(f"ewbG{q}", [128, TQE], F32, stack=phQs, side="right")
            Abuf = ctx.sbuf(f"AbG{q}", [128, TQE + 1], F32, stack=phQs, side="right")
            Bbuf = ctx.sbuf(f"BbG{q}", [128, TQE + 1], F32, stack=phQs, side="right")
            srE = ctx.sbuf(f"srG{q}", [128, TQE], F32, stack=phQs, side="right")
            hopc = ctx.sbuf(f"hopG{q}", [128, 1], F32, stack=phQs, side="right")
            if q == 0:
                ctx.wait("vector", s_gd, c2_done)
            wsKVR.load(0, 0)
            for f in range(8):
                bu = f % 2
                if f < 7:
                    wsKVR.load(f + 1, 1 - bu)
                wb = wsKVR.wb[bu]
                ctx.wait("tensor", s_ve, wsKVR.ready[bu])
                ctx.wait("tensor", s_ve, mix_rdy)
                mm_group(B[0], lambda k, _w=wb: _w[:, k * 128:(k + 1) * 128],
                         lambda k: xkb[k][:], 8, TQE)
                mm_group(B[2], lambda k, _w=wb: _w[:, D + k * 128:D + (k + 1) * 128],
                         lambda k: xvb[k][:], 8, TQE)
                ctx.wait("scalar", s_mm)
                ctx.wait("scalar", s_ve)
                ctx.op("scalar", "activation", ekE[:], B[0].ap(128, TQE), AF.Exp,
                       inc=[(s_sc, 1)])
                B[0].set_guard(s_sc)
                ctx.wait("vector", s_sc)
                ctx.op("vector", "tensor_scalar_mul", eukE[:], ekE[:], eu_col[:, f:f + 1])
                ctx.op("vector", "tensor_mul", ekvE[:], B[2].ap(128, TQE), ekE[:])
                ctx.op("vector", "tensor_mul", evtE[:], B[2].ap(128, TQE), eukE[:],
                       inc=[(s_ve, 1)])
                B[2].set_guard(s_ve)
                if q == 0:
                    ctx.wait("scalar", s_gd, c2_done)
                ctx.op("scalar", "activation", ewbE[:], ekE[:], AF.Identity,
                       scale=0.0, bias=ew_col[:, f:f + 1])
                ctx.op("scalar", "copy", Abuf[:, 0:1],
                       (recvAB[:, f:f + 1] if q == 0 else Afin[:, f:f + 1]))
                ctx.op("scalar", "copy", Bbuf[:, 0:1],
                       (recvAB[:, 8 + f:9 + f] if q == 0 else Bfin[:, f:f + 1]),
                       inc=[(s_sc, 1)])
                ctx.wait("tensor", s_ve, mix_rdy)
                mm_group(B[1], lambda k, _w=wb: _w[:, 2 * D + k * 128:2 * D + (k + 1) * 128],
                         lambda k: xrb[k][:], 8, TQE)
                ctx.wait("vector", s_sc)
                ctx.op("vector", "tensor_tensor_scan", out=Abuf[:, 1:TQE + 1],
                       data0=ewbE[:], data1=ekvE[:], initial=Abuf[:, 0:1],
                       op0=ALU.mult, op1=ALU.add)
                ctx.op("vector", "tensor_tensor_scan", out=Bbuf[:, 1:TQE + 1],
                       data0=ewbE[:], data1=ekE[:], initial=Bbuf[:, 0:1],
                       op0=ALU.mult, op1=ALU.add, inc=[(s_ve, 1)])
                ctx.wait("scalar", s_ve)
                ctx.wait("scalar", s_mm)
                ctx.op("scalar", "copy", Afin[:, f:f + 1], Abuf[:, TQE:TQE + 1])
                ctx.op("scalar", "copy", Bfin[:, f:f + 1], Bbuf[:, TQE:TQE + 1])
                ctx.op("scalar", "activation", srE[:], B[1].ap(128, TQE), AF.Sigmoid,
                       inc=[(s_sc, 1)])
                B[1].set_guard(s_sc)
                ctx.wait("vector", s_sc)
                ctx.op("vector", "tensor_add", eukE[:], Bbuf[:, 0:TQE], eukE[:])
                ctx.op("vector", "tensor_add", evtE[:], Abuf[:, 0:TQE], evtE[:],
                       inc=[(s_ve, 1)])
                ctx.wait("scalar", s_ve)
                ctx.op("scalar", "copy", hopc[:], eukE[:, 0:1], inc=[(s_sc, 1)])
                ctx.wait("vector", s_sc)
                ctx.op("vector", "reciprocal", ekE[:], eukE[:], inc=[(s_ve, 1)])
                ctx.wait("scalar", s_ve)
                ctx.op("scalar", "copy", hopc[:], ekE[:, 0:1], inc=[(s_sc, 1)])
                ctx.wait("vector", s_sc)
                ctx.op("vector", "tensor_mul", evtE[:], evtE[:], srE[:])
                ctx.op("vector", "tensor_mul", rwkvb[f][:], evtE[:], ekE[:],
                       inc=[(s_ve, 1)])
            phQs.close()
            phQ2.close()
            if q > 0:
                gsl = [ctx.sbuf(f"gslS{q}_{f}", [128, 513], BF16, stack=phQ, side="right")
                       for f in range(8)]
            phS1 = ExitStack()
            h0bs = [ctx.sbuf(f"h0bS{q}_{f}", [128, TQE], BF16, stack=phS1, side="right")
                    for f in range(8)]
            xfs = [ctx.sbuf(f"xfS{q}_{f}", [128, TQE], F32, stack=phS1, side="right")
                   for f in range(8)]
            wsSWO = WStream(inp["r_swox"], 2 * D, phS1)
            emit_ln([r_t[f][:, c0:c0 + TQE] for f in range(8)], TQE,
                    [h0bs[f][:] for f in range(8)], lnsc, B[4], B[5], B[3])
            h0_rdy = s_ve.n
            wsSWO.load(0, 0)
            for f in range(8):
                bu = f % 2
                if f < 7:
                    wsSWO.load(f + 1, 1 - bu)
                wb = wsSWO.wb[bu]
                ctx.wait("tensor", s_ve, wsSWO.ready[bu])
                ctx.wait("tensor", s_ve, h0_rdy)
                bk = B[bu]
                bk.mm_wait()
                for k in range(8):
                    ctx.op("tensor", "matmul", bk.ap(128, TQE),
                           wb[:, k * 128:(k + 1) * 128], h0bs[k][:],
                           start=(k == 0), stop=False)
                for k in range(8):
                    ctx.op("tensor", "matmul", bk.ap(128, TQE),
                           wb[:, D + k * 128:D + (k + 1) * 128], rwkvb[k][:],
                           start=False, stop=(k == 7),
                           inc=([(s_mm, 1)] if k == 7 else None))
                wsSWO.used(bu)
                ctx.wait("scalar", s_mm)
                ctx.op("scalar", "copy", xfs[f][:], bk.ap(128, TQE), inc=[(s_sc, 1)])
                bk.set_guard(s_sc)
            if q == 0 and "erwkv0" in debug:
                t = dbg_out("erwkv0", (8, 128, TQE))
                ctx.wait("gpsimd", s_ve)
                for f in range(8):
                    ctx.dma("gpsimd", t[f], rwkvb[f][:], s_st)
            if q == 0 and "exfn0" in debug:
                t = dbg_out("exfn0", (8, 128, TQE))
                ctx.wait("gpsimd", s_sc)
                for f in range(8):
                    ctx.dma("gpsimd", t[f], xfs[f][:], s_st)
            if q == 0:
                gdst = [g0b[f][:, 1:513] for f in range(8)]
            else:
                ctx.wait("vector", s_ve)
                for f in range(8):
                    ctx.op("vector", "tensor_copy", out=gsl[f][:, 0:1],
                           in_=g_last[:, f:f + 1])
                gdst = [gsl[f][:, 1:513] for f in range(8)]
            ctx.wait("scalar", s_sc)
            emit_ln([xfs[f][:] for f in range(8)], TQE, gdst,
                    lnsc, B[4], B[5], B[3],
                    gamma=smalls["r_ln2w"], beta=smalls["r_ln2b"])
            ctx.wait("vector", s_sc)
            ctx.wait("vector", s_ve)
            for f in range(8):
                ctx.op("vector", "tensor_add", r_t[f][:, c0:c0 + TQE],
                       xfs[f][:], r_t[f][:, c0:c0 + TQE],
                       inc=([(s_ve, 1)] if f == 7 else None))
            for f in range(8):
                ctx.op("vector", "tensor_copy", out=g_last[:, f:f + 1],
                       in_=gdst[f][:, 511:512], inc=([(s_ve, 1)] if f == 7 else None))
            phS1.close()
            if q > 0 and SUBE == "full":
                phase_waits(("sync",))
                emit_ffn_slice(c0, gsl, phQ)
            phQ.close()
        if SUBE == "full":
            ctx.wait("vector", s_ve)
            c3_done = allgather_prev(g_last[:], 8, recvh[:])
            phS = ExitStack()
            ctx.wait("vector", s_gd, c3_done)
            for f in range(8):
                ctx.op("vector", "tensor_copy", out=g0b[f][:, 0:1], in_=recvh[:, f:f + 1],
                       inc=([(s_ve, 1)] if f == 7 else None))
            phase_waits(("sync",))
            emit_ffn_slice(0, g0b, phS)
            phS.close()
        phE.close()

    if "r3" in debug:
        t = dbg_out("r3", (8, 128, NT))
        ctx.wait("gpsimd", s_ve)
        for f in range(8):
            ctx.dma("gpsimd", t[f], r_t[f][:], s_st)

    if max_phase >= 8:
        phF = ExitStack()
        phase_waits(("sync", "gpsimd", "scalar"))
        xlnF = [ctx.sbuf(f"xlnF{f}", [128, NT], BF16, stack=phF, side="right")
                for f in range(8)]
        gfm = ctx.sbuf("gfm", [8, NT], BF16, stack=phF, side="right")
        onecb = ctx.sbuf("onecb", [1, 128], BF16, stack=phF, side="right")
        ctx.op("vector", "memset", onecb[:], 1.0, inc=[(s_ve, 1)])
        phF1 = ExitStack()
        lnsc = make_ln_scratch(phF1)
        for sl in range(4):
            c0 = sl * 512
            emit_ln([r_t[f][:, c0:c0 + 512] for f in range(8)], 512,
                    [xlnF[f][:, c0:c0 + 512] for f in range(8)],
                    lnsc, B[4], B[5], B[3])
        wgb = ctx.sbuf("wgb", [128, 64], BF16, stack=phF1, side="right")
        bgb = ctx.sbuf("bgb", [128, 128], F32, stack=phF1, side="right")
        identF = ctx.sbuf("identF", [128, 128], F32, stack=phF1, side="right")
        gsc = ctx.sbuf("gsc", [128, 5 * 128], F32, stack=phF1, side="right")
        lgt = gsc[:, 0:128]
        p8 = gsc[:, 128:256]
        msk = gsc[:, 256:384]
        grows = gsc[:, 384:384 + 64]
        getm = gsc[:, 448:576]
        if wstage_guard[0] is not None:
            ctx.wait("sync", s_ve, wstage_guard[0])
        ctx.dma("sync", wstage[:, 0:64], inp["m_wgx"][:], s_ld)
        ctx.dma("sync", bgb[:], inp["m_bgx"][:], s_ld)
        ctx.dma("sync", identF[:], inp["ident"][:], s_ld)
        ctx.wait("vector", s_ld)
        ctx.op("vector", "tensor_copy", out=wgb[:], in_=wstage[:, 0:64], inc=[(s_ve, 1)])
        wstage_guard[0] = s_ve.n
        gb = B[6]
        gb.mm_wait()
        ctx.wait("tensor", s_ve)
        for tt in range(16):
            for k in range(8):
                ctx.op("tensor", "matmul", gb.ap(128, 8, off=tt * 8),
                       xlnF[k][:, tt * 128:(tt + 1) * 128], wgb[:, k * 8:(k + 1) * 8],
                       start=(tt == 0 and k == 0), stop=(tt == 15 and k == 7),
                       inc=([(s_mm, 1)] if (tt == 15 and k == 7) else None))
        ctx.wait("vector", s_mm)
        ctx.op("vector", "tensor_add", lgt[:], gb.ap(128, 128), bgb[:], inc=[(s_ve, 1)])
        gb.set_guard(s_ve)
        ctx.op("vector", "reduce_max",
               grows[:, 0:16], lgt.rearrange("p (t e) -> p t e", e=8), axis=AX.X,
               negate=True, inc=[(s_ve, 1)])
        ctx.wait("scalar", s_ve)
        for tt in range(16):
            ctx.op("scalar", "activation", p8[:, tt * 8:(tt + 1) * 8],
                   lgt[:, tt * 8:(tt + 1) * 8], AF.Exp,
                   bias=grows[:, tt:tt + 1], inc=([(s_sc, 1)] if tt == 15 else None))
        ctx.wait("vector", s_sc)
        ctx.op("vector", "reduce_sum",
               grows[:, 16:32], p8.rearrange("p (t e) -> p t e", e=8), axis=AX.X,
               inc=[(s_ve, 1)])
        ctx.wait("scalar", s_ve)
        ctx.op("scalar", "copy", grows[:, 32:48], grows[:, 16:32], inc=[(s_sc, 1)])
        ctx.wait("vector", s_sc)
        ctx.op("vector", "reciprocal", grows[:, 16:32], grows[:, 32:48],
               inc=[(s_ve, 1)])
        ctx.op("vector", "reduce_max",
               grows[:, 32:48], p8.rearrange("p (t e) -> p t e", e=8), axis=AX.X,
               inc=[(s_ve, 1)])
        ctx.wait("gpsimd", s_ve)
        for tt in range(16):
            ctx.op("gpsimd", "tensor_single_scalar", msk[:, tt * 8:(tt + 1) * 8],
                   p8[:, tt * 8:(tt + 1) * 8], grows[:, 32 + tt:33 + tt],
                   op=ALU.is_lt)
        ctx.op("gpsimd", "engine_nop", inc=[(s_gp, 1)])
        ctx.wait("vector", s_gp)
        ctx.op("vector", "tensor_mul", msk[:], p8[:], msk[:], inc=[(s_ve, 1)])
        ctx.op("vector", "reduce_max",
               grows[:, 48:64], msk.rearrange("p (t e) -> p t e", e=8), axis=AX.X,
               inc=[(s_ve, 1)])
        ctx.wait("gpsimd", s_ve)
        for tt in range(16):
            ctx.op("gpsimd", "tensor_single_scalar", msk[:, tt * 8:(tt + 1) * 8],
                   p8[:, tt * 8:(tt + 1) * 8], grows[:, 48 + tt:49 + tt],
                   op=ALU.is_ge)
        ctx.op("gpsimd", "engine_nop", inc=[(s_gp, 1)])
        ctx.wait("vector", s_gp)
        ctx.op("vector", "tensor_mul", getm[:], p8[:], msk[:])
        for tt in range(16):
            ctx.op("vector", "tensor_scalar_mul", getm[:, tt * 8:(tt + 1) * 8],
                   getm[:, tt * 8:(tt + 1) * 8], grows[:, 16 + tt:17 + tt],
                   inc=([(s_ve, 1)] if tt == 15 else None))
        for tt in range(16):
            tb = B[7]
            tb.mm_wait()
            ctx.wait("tensor", s_ve)
            ctx.op("tensor", "transpose", tb.ap(8, 128), getm[:, tt * 8:(tt + 1) * 8],
                   identF[:], inc=[(s_mm, 1)])
            ctx.wait("scalar", s_mm)
            ctx.op("scalar", "copy", gfm[0:8, tt * 128:(tt + 1) * 128], tb.ap(8, 128),
                   inc=[(s_sc, 1)])
            tb.set_guard(s_sc)
        if "egate" in debug:
            t = dbg_out("egate", (8, NT))
            ctx.wait("gpsimd", s_sc)
            ctx.dma("gpsimd", t[:], gfm[:], s_st)
        phF1.close()

        TM = NT // 2
        gebF = ctx.sbuf("gebF", [128, TM], BF16, stack=phF, side="right")
        gerow = ctx.sbuf("gerow", [1, NT], BF16, stack=phF, side="right")
        HqF = ctx.sbuf("HqF", [128, 32 * TM], BF16, stack=phF, side="right")
        tmpF = [ctx.sbuf(f"tmpF{i}", [128, 512], F32, stack=phF, side="right")
                for i in range(2)]
        b1F = ctx.sbuf("b1F", [128, NE * 32], F32, stack=phF, side="right")
        b2F = ctx.sbuf("b2F", [128, NE * 8], F32, stack=phF, side="right")
        ctx.dma("sync", b1F[:], inp["m_b1x"][:], s_ld)
        ctx.dma("sync", b2F[:], inp["m_b2x"][:], s_ld)
        wsW1 = WStream(inp["m_w1x"][0], D, phF, stg_cols=512)
        wsW2 = WStream(inp["m_w2x"][0], DH, phF, stg_cols=1024)
        ctx.wait("tensor", s_ld)
        xln_rdy = s_ve.n
        ge_guard = [None]
        for e in range(NE):
            wsW1.dram = inp["m_w1x"][e]
            wsW2.dram = inp["m_w2x"][e]
            ctx.wait("gpsimd", s_sc)
            ctx.dma("gpsimd", gerow[:], gfm[e:e + 1, :], s_gd)
            for th in (0, 1):
                t0 = th * TM
                bb = B[6]
                bb.mm_wait()
                B[7].mm_wait()
                ctx.wait("tensor", s_gd)
                ctx.op("tensor", "matmul", bb.ap(128, 512, off=0), onecb[:],
                       gerow[:, t0:t0 + 512], start=True, stop=True)
                ctx.op("tensor", "matmul", PSD[0:128, 0:512], onecb[:],
                       gerow[:, t0 + 512:t0 + 1024], start=True, stop=True,
                       inc=[(s_mm, 1)])
                ctx.wait("scalar", s_mm)
                ctx.wait("scalar", s_ve)
                ctx.op("scalar", "copy", gebF[:, 0:512], bb.ap(128, 512))
                ctx.op("scalar", "copy", gebF[:, 512:1024], PSD[0:128, 0:512],
                       inc=[(s_sc, 1)])
                bb.set_guard(s_sc)
                B[7].set_guard(s_sc)
                wsW1.load(0, 0)
                for oc in range(32):
                    bu = oc % 2
                    if oc < 31:
                        wsW1.load(oc + 1, 1 - bu)
                    wb = wsW1.wb[bu]
                    ctx.wait("tensor", s_ve, wsW1.ready[bu])
                    ctx.wait("tensor", s_ve, xln_rdy)
                    for i in range(2):
                        mm_group(B[2 * bu + i],
                                 lambda k, _w=wb: _w[:, k * 128:(k + 1) * 128],
                                 lambda k, _c=t0 + i * 512: xlnF[k][:, _c:_c + 512],
                                 8, 512)
                    wsW1.used(bu)
                    ctx.wait("scalar", s_mm)
                    for i in range(2):
                        ctx.op("scalar", "activation",
                               HqF[:, oc * TM + i * 512:oc * TM + (i + 1) * 512],
                               B[2 * bu + i].ap(128, 512), AF.Gelu,
                               bias=b1F[:, e * 32 + oc:e * 32 + oc + 1],
                               inc=([(s_sc, 1)] if i == 1 else None))
                        B[2 * bu + i].set_guard(s_sc)
                hq_rdy = s_sc.n
                wsW2.load(0, 0)
                for f in range(8):
                    bu = f % 2
                    if f < 7:
                        wsW2.load(f + 1, 1 - bu)
                    wb = wsW2.wb[bu]
                    ctx.wait("tensor", s_ve, wsW2.ready[bu])
                    ctx.wait("tensor", s_sc, hq_rdy)
                    for i in range(2):
                        mm_group(B[2 * bu + i],
                                 lambda k, _w=wb: _w[:, k * 128:(k + 1) * 128],
                                 lambda k, _i=i: HqF[:, k * TM + _i * 512:k * TM + _i * 512 + 512],
                                 32, 512)
                    wsW2.used(bu)
                    ctx.wait("scalar", s_mm)
                    ctx.wait("scalar", s_ve)
                    for i in range(2):
                        ctx.op("scalar", "activation", tmpF[i][:],
                               B[2 * bu + i].ap(128, 512), AF.Identity,
                               bias=b2F[:, e * 8 + f:e * 8 + f + 1],
                               inc=([(s_sc, 1)] if i == 1 else None))
                        B[2 * bu + i].set_guard(s_sc)
                    ctx.wait("vector", s_sc)
                    for i in range(2):
                        c0 = t0 + i * 512
                        ctx.op("vector", "tensor_mul", tmpF[i][:], tmpF[i][:],
                               gebF[:, i * 512:(i + 1) * 512])
                        ctx.op("vector", "tensor_add", r_t[f][:, c0:c0 + 512],
                               tmpF[i][:], r_t[f][:, c0:c0 + 512],
                               inc=([(s_ve, 1)] if i == 1 else None))
        phF.close()

    ctx.wait("gpsimd", s_ve)
    for f in range(8):
        ctx.dma("gpsimd", out[f], r_t[f][:], s_st)


    ctx.wait("gpsimd", s_st)
    ctx.wait("sync", s_st)
    return



def _chunk_vec(v):
    d = v.shape[0]
    return np.ascontiguousarray(v.reshape(d // 128, 128).T)


def _rope_tables():
    pos = np.arange(S, dtype=np.float64)
    ids = np.arange(D // 2, dtype=np.float64)
    theta = (10000.0 ** (-2.0 * ids)) / (D // 2)
    r = pos[None, :] * theta[:, None]
    cosf = np.repeat(np.cos(r).astype(np.float32), 2, axis=0)
    sinf = np.repeat(np.sin(r).astype(np.float32), 2, axis=0)
    return cosf, sinf


def _perm_pm():
    P = np.zeros((128, 128), np.float32)
    for i in range(64):
        P[2 * i + 1, 2 * i] = -1.0
        P[2 * i, 2 * i + 1] = 1.0
    return P


def stage_inputs(inp):
    T = np.ascontiguousarray
    A = lambda k: np.asarray(inp[k], dtype=np.float32)
    cosf, sinf = _rope_tables()
    x = A("L0_attn_x")

    sh = {}
    sh["a_wdq"] = T(A("L0_attn_Wd_q_w"))
    sh["a_bdq"] = _chunk_vec(A("L0_attn_Wd_q_b"))
    sh["a_wuq"] = T(A("L0_attn_Wu_q_w"))
    sh["a_buq"] = _chunk_vec(A("L0_attn_Wu_q_b"))
    sh["a_wqr"] = T(A("L0_attn_Wq_r_w"))
    sh["a_bqr"] = _chunk_vec(A("L0_attn_Wq_r_b"))
    sh["a_wkr"] = T(A("L0_attn_Wk_r_w"))
    sh["a_bkr"] = _chunk_vec(A("L0_attn_Wk_r_b"))
    sh["a_wdkv"] = T(A("L0_attn_Wd_kv_w"))
    sh["a_bdkv"] = _chunk_vec(A("L0_attn_Wd_kv_b"))
    sh["a_wuk"] = T(A("L0_attn_Wu_k_w"))
    sh["a_buk"] = _chunk_vec(A("L0_attn_Wu_k_b"))
    sh["a_wuv"] = T(A("L0_attn_Wu_v_w"))
    sh["a_buv"] = T(A("L0_attn_Wu_v_b")[None, :])
    sh["ident"] = np.eye(128, dtype=np.float32)
    sh["a_wo"] = T(A("L0_attn_Wo_w"))
    sh["a_bo"] = _chunk_vec(A("L0_attn_Wo_b"))
    sh["perm"] = _perm_pm()

    def lhsT_oc(Wt):
        K, M = Wt.shape
        return T(Wt.reshape(K // 128, 128, M // 128, 128)
                 .transpose(2, 1, 0, 3).reshape(M // 128, 128, K))

    w12 = np.concatenate([lhsT_oc(A("L0_ffn_w1_w").T),
                          lhsT_oc(A("L0_ffn_w2_w").T)], axis=2)
    sh["f_w12x"] = T(w12)
    sh["f_s1"] = _chunk_vec(A("L0_ffn_w1_s"))
    sh["f_s2"] = _chunk_vec(A("L0_ffn_w2_s"))
    sh["f_h1x"] = lhsT_oc(A("L0_ffn_h1_w").T)
    sh["f_sh1"] = _chunk_vec(A("L0_ffn_h1_s"))
    sh["f_h2x"] = lhsT_oc(A("L0_ffn_h2_w").T)
    sh["f_sh2"] = _chunk_vec(A("L0_ffn_h2_s"))
    sh["f_w3x"] = lhsT_oc(A("L0_ffn_w3_w").T)
    sh["f_s3"] = _chunk_vec(A("L0_ffn_w3_s"))

    sh["r_ln1w"] = _chunk_vec(A("L1_attn_ln1_w"))
    sh["r_ln1b"] = _chunk_vec(A("L1_attn_ln1_b"))
    sh["r_ln2w"] = _chunk_vec(A("L1_attn_ln2_w"))
    sh["r_ln2b"] = _chunk_vec(A("L1_attn_ln2_b"))
    sh["r_decay"] = _chunk_vec(A("L1_attn_att_time_decay"))
    sh["r_decay_row"] = T(A("L1_attn_att_time_decay")[None, :])
    sh["r_first"] = _chunk_vec(A("L1_attn_att_time_first"))
    sh["r_mixk"] = _chunk_vec(A("L1_attn_att_mix_k"))
    sh["r_mixv"] = _chunk_vec(A("L1_attn_att_mix_v"))
    sh["r_mixr"] = _chunk_vec(A("L1_attn_att_mix_r"))
    sh["r_wkvrx"] = T(np.concatenate([lhsT_oc(A("L1_attn_att_Wk").T),
                                      lhsT_oc(A("L1_attn_att_Wv").T),
                                      lhsT_oc(A("L1_attn_att_Wr").T)], axis=2))
    sh["r_swox"] = T(np.concatenate([lhsT_oc(A("L1_attn_short_W").T),
                                     lhsT_oc(A("L1_attn_att_Wo").T)], axis=2))
    sh["r_fmixk"] = _chunk_vec(A("L1_attn_ffn_mix_k"))
    sh["r_fmixr"] = _chunk_vec(A("L1_attn_ffn_mix_r"))
    sh["r_fwkx"] = lhsT_oc(A("L1_attn_ffn_Wk").T)
    sh["r_fwvrx"] = T(np.concatenate([lhsT_oc(A("L1_attn_ffn_Wv").T),
                                      lhsT_oc(A("L1_attn_ffn_Wr").T)], axis=2))

    wg = A("L1_ffn_Wg")
    sh["m_wgx"] = T(wg.reshape(8, 128, 8).transpose(1, 0, 2).reshape(128, 64))
    sh["m_bgx"] = T(np.tile(A("L1_ffn_bg"), (128, 16)))
    w1 = A("L1_ffn_W1")
    sh["m_w1x"] = T(w1.reshape(8, 8, 128, 32, 128).transpose(0, 3, 2, 1, 4)
                    .reshape(8, 32, 128, 1024))
    b1 = A("L1_ffn_b1")
    sh["m_b1x"] = T(np.concatenate([_chunk_vec(b1[e]) for e in range(NE)], 1))
    w2 = A("L1_ffn_W2")
    sh["m_w2x"] = T(w2.reshape(8, 32, 128, 8, 128).transpose(0, 3, 2, 1, 4)
                    .reshape(8, 8, 128, 4096))
    b2 = A("L1_ffn_b2")
    sh["m_b2x"] = T(np.concatenate([_chunk_vec(b2[e]) for e in range(NE)], 1))
    sh["iota_rev"] = T(np.arange(511, -1, -1, dtype=np.float32)[None, :])

    in_maps = []
    for c in range(NCORES):
        b, hh = c // 2, c % 2
        own = x[b, hh * NT:(hh + 1) * NT]
        oth = x[b, (1 - hh) * NT:(1 - hh) * NT + NT]
        xT = np.concatenate([own, oth], 0).T.astype(np.float32)
        posk = np.concatenate([np.arange(hh * NT, (hh + 1) * NT),
                               np.arange((1 - hh) * NT, (1 - hh) * NT + NT)])
        m = dict(sh)
        m["x_phA"] = T(xT.reshape(8, 128, S))
        m["x_own"] = T(xT[:, 0:NT])
        m["cosk"] = T(cosf[:, posk])
        m["sink"] = T(sinf[:, posk])
        mask = np.zeros((8, 1), np.float32)
        if hh == 1:
            mask[c - 1, 0] = 1.0
        m["mask8"] = mask
        in_maps.append(m)
    return in_maps


def kernel(**inputs):
    in_maps = stage_inputs(inputs)
    nc = build()
    used = nc._used_inputs
    in_maps = [{k: v for k, v in m.items() if k in used} for m in in_maps]
    res = run_bass_kernel_spmd(nc, in_maps, core_ids=list(range(NCORES)))
    outp = np.empty((4, S, D), np.float32)
    for c in range(NCORES):
        b, hh = c // 2, c % 2
        r = res.results[c]["out"]
        outp[b, hh * NT:(hh + 1) * NT] = r.reshape(D, NT).T
    return outp


# version 07 (final)
# speedup vs baseline: 1.1267x; 1.1267x over previous
import math
import traceback
import numpy as np
import concourse.bass as bass
import concourse.mybir as mybir
from concourse.bass_utils import run_bass_kernel_spmd
from contextlib import ExitStack

F32 = mybir.dt.float32
BF16 = mybir.dt.bfloat16
AF = mybir.ActivationFunctionType
ALU = mybir.AluOpType
AX = mybir.AxisListType

NCORES = 8
D = 1024
S = 4096
NT = 2048
NH = 16
HD = 64
DQ = 512
DKV = 512
DH = 4096
NE = 8
SCALE = math.sqrt(NH + DQ + DKV)
EPS = 1e-5

ENGINES = ("sync", "vector", "scalar", "tensor", "gpsimd")
SUB = "full"
SUBE = "full"


class Sem:
    __slots__ = ("h", "n", "name")

    def __init__(self, h, name):
        self.h = h
        self.n = 0
        self.name = name


class Ctx:

    def __init__(self, nc, stack):
        self.nc = nc
        self.stack = stack
        self.ops = {e: [] for e in ENGINES}

    def sem(self, name):
        return Sem(self.stack.enter_context(self.nc.semaphore(name)), name)

    def sbuf(self, name, shape, dtype, stack=None, side=None):
        return (stack or self.stack).enter_context(
            self.nc.sbuf_tensor(name, shape, dtype, side=side))

    def psum(self, name, shape, dtype=F32):
        return self.stack.enter_context(self.nc.psum_tensor(name, shape, dtype))

    def op(self, eng, meth, *args, inc=None, **kw):
        incs = []
        if inc:
            for s, k in inc:
                s.n += k
                incs.append((s, k))
        frs = traceback.extract_stack(limit=5)[:-1]
        loc = ">".join(f"{f.name}:{f.lineno}" for f in frs[-3:])

        def thunk(e, _m=meth, _a=args, _kw=kw, _incs=incs, _loc=loc):
            instr = getattr(e, _m)(*_a, **_kw)
            try:
                instr.annotate(_loc)
            except Exception:
                pass
            for s, k in _incs:
                instr = instr.then_inc(s.h, k)
            return instr

        self.ops[eng].append(thunk)

    def wait(self, eng, sem, val=None):
        v = sem.n if val is None else val
        self.ops[eng].append(lambda e, _s=sem, _v=v: e.wait_ge(_s.h, _v))

    def dma(self, eng, out, in_, sem=None):
        inc = [(sem, 16)] if sem is not None else None
        self.op(eng, "dma_start", inc=inc, out=out, in_=in_)

    def replay(self):
        with self.nc.Block() as block:
            for e in ENGINES:
                getattr(block, e)(lambda eng, _e=e: [f(eng) for f in self.ops[_e]])


class Bank:

    def __init__(self, ctx, t, base=0, width=512):
        self.t = t
        self.base = base
        self.width = width
        self.guard = None
        self.ctx = ctx

    def mm_wait(self, eng="tensor"):
        if self.guard is not None:
            self.ctx.wait(eng, self.guard[0], self.guard[1])

    def set_guard(self, sem):
        self.guard = (sem, sem.n)

    def ap(self, p=128, n=None, off=0, p0=0):
        n = self.width if n is None else n
        return self.t[p0:p0 + p, self.base + off:self.base + off + n]


def build(debug=(), max_phase=99):
    nc = bass.Bass("TRN2", target_bir_lowering=False, debug=False,
                   detect_race_conditions=False)
    stack = ExitStack()
    ctx = Ctx(nc, stack)

    shapes = {
        "x_phA": (8, 128, S), "x_own": (D, NT),
        "cosk": (D, S), "sink": (D, S), "mask8": (8, 1),
        "a_wdq": (D, DQ), "a_bdq": (128, 4), "a_wuq": (DQ, D), "a_buq": (128, 8),
        "a_wqr": (DQ, D), "a_bqr": (128, 8), "a_wkr": (D, D), "a_bkr": (128, 8),
        "a_wdkv": (D, DKV), "a_bdkv": (128, 4), "a_wuk": (DKV, D), "a_buk": (128, 8),
        "a_wuv": (DKV, D), "a_buv": (1, D), "ident": (128, 128),
        "a_wo": (D, D), "a_bo": (128, 8), "perm": (128, 128),
        "f_w12x": (32, 128, 2048), "f_s1": (128, 32), "f_s2": (128, 32),
        "f_h1x": (32, 128, DH), "f_sh1": (128, 32),
        "f_h2x": (32, 128, DH), "f_sh2": (128, 32),
        "f_w3x": (8, 128, DH), "f_s3": (128, 8),
        "r_ln1w": (128, 8), "r_ln1b": (128, 8), "r_ln2w": (128, 8), "r_ln2b": (128, 8),
        "r_decay": (128, 8), "r_decay_row": (1, D), "r_first": (128, 8),
        "r_mixk": (128, 8), "r_mixv": (128, 8), "r_mixr": (128, 8),
        "r_wkvrx": (8, 128, 3 * D), "r_swox": (8, 128, 2 * D),
        "r_fmixk": (128, 8), "r_fmixr": (128, 8),
        "r_fwkx": (32, 128, D), "r_fwvrx": (8, 128, DH + D),
        "m_wgx": (128, 64), "m_bgx": (128, 128),
        "m_w1x": (NE, 32, 128, D), "m_b1x": (128, NE * 32),
        "m_w2x": (NE, 8, 128, DH), "m_b2x": (128, NE * 8),
        "iota_rev": (1, 512),
    }
    class LazyInp(dict):
        def __missing__(self, k):
            ap = nc.dram_tensor(k, list(shapes[k]), F32, kind="ExternalInput").ap()
            self[k] = ap
            return ap
    inp = LazyInp()
    out = nc.dram_tensor("out", [8, 128, NT], F32, kind="ExternalOutput").ap()

    dbg = {}

    def dbg_out(name, shape):
        t = nc.dram_tensor("dbg_" + name, list(shape), F32, kind="ExternalOutput").ap()
        dbg[name] = t
        return t

    with stack:
        build_body(ctx, inp, out, dbg_out, debug, max_phase)
        ctx.replay()
    nc._used_inputs = set(inp.keys())
    return nc


def build_body(ctx, inp, out, dbg_out, debug, max_phase):
    REG = [ctx.psum(f"reg{i}", [128, 1024], F32) for i in range(3)]
    PSA = ctx.psum("psa", [128, 512], F32)
    PSD = ctx.psum("psd", [128, 512], F32)
    B = [Bank(ctx, REG[i // 2], (i % 2) * 512) for i in range(6)] + \
        [Bank(ctx, PSA), Bank(ctx, PSD)]
    s_ld = ctx.sem("s_ld")
    s_xd = ctx.sem("s_xd")
    s_gd = ctx.sem("s_gd")
    s_mm = ctx.sem("s_mm")
    s_ve = ctx.sem("s_ve")
    s_sc = ctx.sem("s_sc")
    s_st = ctx.sem("s_st")
    s_gp = ctx.sem("s_gp")

    ones_col = ctx.sbuf("ones_col", [1, 128], F32)
    ones128 = ctx.sbuf("ones128", [128, 1], F32)
    ones128b = ctx.sbuf("ones128b", [128, 1], BF16)
    ctx.op("vector", "memset", ones_col[:], 1.0)
    ctx.op("vector", "memset", ones128b[:], 1.0)
    ctx.op("vector", "memset", ones128[:], 1.0, inc=[(s_ve, 1)])

    smalls = {}
    for nm in ("a_bdq", "a_bdkv", "a_buk", "a_bkr", "a_buq", "a_bqr", "a_bo",
               "r_ln1w", "r_ln1b", "r_ln2w", "r_ln2b", "r_decay",
               "r_first", "r_mixk", "r_mixv", "r_mixr", "r_fmixk", "r_fmixr",
               "f_s1", "f_s2", "f_sh1", "f_sh2", "f_s3", "mask8"):
        t = ctx.sbuf(nm + "_s", list(inp[nm].shape), F32)
        ctx.dma("sync", t[:], inp[nm][:], s_ld)
        smalls[nm] = t

    def load_small(nm, stack):
        t = ctx.sbuf(nm + "_s", list(inp[nm].shape), F32, stack=stack)
        ctx.dma("sync", t[:], inp[nm][:], s_ld)
        smalls[nm] = t
        return t

    wstage = ctx.sbuf("wstage", [128, 1024], F32)
    wstage_guard = [None]

    def load_weight_bf16(name, ap, K, M, stack=None, eng="sync", side=None):
        wb = ctx.sbuf(name + "_b", [128, (K // 128) * M], BF16, stack=stack, side=side)
        piece = max(1, 1024 // M)
        for k0 in range(0, K // 128, piece):
            kk = min(piece, K // 128 - k0)
            if wstage_guard[0] is not None:
                ctx.wait(eng, s_ve, wstage_guard[0])
            src = ap.rearrange("(k p) m -> p k m", p=128)[:, k0:k0 + kk, :]
            ctx.dma(eng, wstage[:, 0:kk * M].rearrange("p (k m) -> p k m", k=kk) if kk > 1 else wstage[:, 0:M], src if kk > 1 else ap[k0 * 128:(k0 + 1) * 128, :], s_ld)
            ctx.wait("vector", s_ld)
            ctx.op("vector", "tensor_copy", inc=[(s_ve, 1)],
                   out=wb[:, k0 * M:(k0 + kk) * M], in_=wstage[:, 0:kk * M])
            wstage_guard[0] = s_ve.n
        return wb

    TCH = 512
    NCH = S // TCH
    phA = ExitStack()
    w_dkv = load_weight_bf16("a_wdkv", inp["a_wdkv"], D, DKV, stack=phA, side="right")
    w_dq = load_weight_bf16("a_wdq", inp["a_wdq"], D, DQ, stack=phA, side="right")
    w_uk = load_weight_bf16("a_wuk", inp["a_wuk"], DKV, D, stack=phA, side="right")
    w_kr = load_weight_bf16("a_wkr", inp["a_wkr"], D, D, stack=phA, side="right")
    permb = ctx.sbuf("permb", [128, 128], BF16)
    if wstage_guard[0] is not None:
        ctx.wait("sync", s_ve, wstage_guard[0])
    ctx.dma("sync", wstage[:, 0:128], inp["perm"][:], s_ld)
    ctx.wait("vector", s_ld)
    ctx.op("vector", "tensor_copy", out=permb[:], in_=wstage[:, 0:128], inc=[(s_ve, 1)])
    wstage_guard[0] = s_ve.n

    phAB = ExitStack()
    ctkv = [ctx.sbuf(f"ctkv{f}", [128, S], BF16, stack=phAB) for f in range(4)]
    Kp = [ctx.sbuf(f"Kp{p}", [128, S], BF16, stack=phAB) for p in range(8)]
    phQ = ExitStack()
    ctq = [ctx.sbuf(f"ctq{f}", [128, NT], BF16, stack=phQ) for f in range(4)]

    xf = ctx.sbuf("xf", [128, 8 * TCH], F32, stack=phA, side="right")
    xf_t = [xf[:, f * TCH:(f + 1) * TCH] for f in range(8)]
    xln = ctx.sbuf("xln", [128, 8 * TCH], BF16, stack=phA, side="right")
    xln_t = [xln[:, f * TCH:(f + 1) * TCH] for f in range(8)]
    cosb = [ctx.sbuf("cosb0", [128, TCH], F32, stack=phA, side="right")] * 2
    sinb = [ctx.sbuf("sinb0", [128, TCH], F32, stack=phA, side="right")] * 2
    m1b = [ctx.sbuf(f"m1b{i}", [128, TCH], BF16, stack=phA, side="right") for i in range(2)]
    sqb = [ctx.sbuf("sqb0", [128, TCH], F32, stack=phA, side="right")] * 2
    mub = ctx.sbuf("mub", [128, TCH], F32, stack=phA, side="right")
    rsb = ctx.sbuf("rsb", [128, TCH], F32, stack=phA, side="right")
    tt0 = ctx.sbuf("tt0", [128, TCH], F32, stack=phA, side="right")
    tt1 = ctx.sbuf("tt1", [128, TCH], F32, stack=phA, side="right")
    rowb = ctx.sbuf("rowb", [1, 4 * TCH], F32, stack=phA, side="right")

    s_rope = ctx.sem("s_rope")
    s_cons = ctx.sem("s_cons")

    def rsqrt_row(va_row, rs_row, t1_row):
        ctx.op("vector", "reciprocal", t1_row, va_row, inc=[(s_ve, 1)])
        ctx.wait("scalar", s_ve)
        ctx.op("scalar", "activation", rs_row, t1_row, AF.Sqrt, inc=[(s_sc, 1)])
        ctx.wait("vector", s_sc)
        ctx.op("vector", "tensor_mul", t1_row, rs_row, rs_row)
        ctx.op("vector", "tensor_mul", t1_row, t1_row, va_row)
        ctx.op("vector", "tensor_scalar", t1_row, t1_row, -1.0, 3.0,
               op0=ALU.mult, op1=ALU.add)
        ctx.op("vector", "tensor_mul", t1_row, t1_row, rs_row)
        ctx.op("vector", "tensor_scalar_mul", rs_row, t1_row, 0.5, inc=[(s_ve, 1)])

    def ln_plain(src_t, T, out_t, sxb, sx2b, bcb):
        mu_row = rowb[:, 0:T]
        va_row = rowb[:, TCH:TCH + T]
        rs_row = rowb[:, 2 * TCH:2 * TCH + T]
        t1_row = rowb[:, 3 * TCH:3 * TCH + T]
        sxb.mm_wait()
        for f in range(8):
            ctx.op("tensor", "matmul", sxb.ap(1, T), ones128[:], src_t[f],
                   start=(f == 0), stop=(f == 7), inc=([(s_mm, 1)] if f == 7 else None))
        sx2b.mm_wait()
        for f in range(8):
            if len(sq_guard) >= 1:
                ctx.wait("scalar", s_mm, val=sq_guard[-1])
            ctx.op("scalar", "activation", sqb[f % 2][:, 0:T], src_t[f], AF.Square,
                   inc=[(s_sc, 1)])
            ctx.wait("tensor", s_sc)
            ctx.op("tensor", "matmul", sx2b.ap(1, T), ones128[:], sqb[f % 2][:, 0:T],
                   start=(f == 0), stop=(f == 7), inc=[(s_mm, 1)])
            sq_guard.append(s_mm.n)
        ctx.wait("vector", s_mm)
        ctx.op("vector", "tensor_scalar_mul", mu_row, sxb.ap(1, T), 1.0 / D)
        ctx.op("vector", "tensor_scalar_mul", t1_row, sx2b.ap(1, T), 1.0 / D,
               inc=[(s_ve, 1)])
        sxb.set_guard(s_ve)
        sx2b.set_guard(s_ve)
        ctx.op("vector", "tensor_mul", va_row, mu_row, mu_row)
        ctx.op("vector", "tensor_sub", va_row, t1_row, va_row)
        ctx.op("vector", "tensor_scalar_add", va_row, va_row, EPS)
        rsqrt_row(va_row, rs_row, t1_row)
        bcb.mm_wait()
        sxb.mm_wait()
        ctx.wait("tensor", s_ve)
        ctx.op("tensor", "matmul", bcb.ap(128, T, 0), ones_col[:], mu_row,
               start=True, stop=True)
        ctx.op("tensor", "matmul", sxb.ap(128, T, 0), ones_col[:], rs_row,
               start=True, stop=True, inc=[(s_mm, 1)])
        ctx.wait("scalar", s_mm)
        ctx.op("scalar", "copy", mub[:, 0:T], bcb.ap(128, T, 0))
        ctx.op("scalar", "copy", rsb[:, 0:T], sxb.ap(128, T, 0), inc=[(s_sc, 1)])
        bcb.set_guard(s_sc)
        sxb.set_guard(s_sc)
        ctx.wait("vector", s_sc)
        for f in range(8):
            ctx.op("vector", "tensor_sub", tt0[:, 0:T], src_t[f], mub[:, 0:T])
            ctx.op("vector", "tensor_mul", out_t[f], tt0[:, 0:T], rsb[:, 0:T],
                   inc=([(s_ve, 1)] if f == 7 else None))

    sq_guard = []

    def mm_group(bank, lhs_fn, rhs_fn, kchunks, n):
        bank.mm_wait()
        for k in range(kchunks):
            ctx.op("tensor", "matmul", bank.ap(128, n), lhs_fn(k), rhs_fn(k),
                   start=(k == 0), stop=(k == kchunks - 1),
                   inc=([(s_mm, 1)] if k == kchunks - 1 else None))

    def evac(bank, out_sbuf, n, func=AF.Identity, bias=None, scale=None):
        ctx.wait("scalar", s_mm)
        kw = {}
        if bias is not None:
            kw["bias"] = bias
        if scale is not None:
            kw["scale"] = scale
        ctx.op("scalar", "activation", out_sbuf, bank.ap(128, n), func,
               inc=[(s_sc, 1)], **kw)
        bank.set_guard(s_sc)

    if max_phase >= 1:
        rope_guard = [None, None]
        m1_guard = [None, None]
        for t in range(NCH):
            c0 = t * TCH
            if t > 0:
                ctx.wait("scalar", s_ve)
                ctx.wait("scalar", s_mm)
            ctx.dma("scalar", xf[:].rearrange("p (f t) -> p f t", f=8),
                    inp["x_phA"][:, :, c0:c0 + TCH].transpose([1, 0, 2]), s_xd)
            ctx.wait("tensor", s_xd)
            ctx.wait("vector", s_xd)
            ctx.wait("scalar", s_xd)
            if t == 0:
                ctx.wait("tensor", s_ve)
            ln_plain(xf_t, TCH, xln_t, B[6], B[5], B[7])

            ctx.wait("tensor", s_ve)
            for oc in range(4):
                mm_group(B[oc % 2], lambda k, _o=oc: w_dkv[:, k * DKV + _o * 128:k * DKV + _o * 128 + 128],
                         lambda k: xln_t[k], 8, TCH)
                evac(B[oc % 2], ctkv[oc][:, c0:c0 + TCH], TCH,
                     bias=smalls["a_bdkv"][:, oc:oc + 1])
            if t < NT // TCH:
                for oc in range(4):
                    mm_group(B[oc % 2], lambda k, _o=oc: w_dq[:, k * DQ + _o * 128:k * DQ + _o * 128 + 128],
                             lambda k: xln_t[k], 8, TCH)
                    evac(B[oc % 2], ctq[oc][:, c0:c0 + TCH], TCH,
                         bias=smalls["a_bdq"][:, oc:oc + 1])
            for oc in range(8):
                pb = oc % 2
                if rope_guard[0] is not None:
                    ctx.wait("gpsimd", s_ve, rope_guard[0])
                ctx.dma("gpsimd", cosb[pb][:],
                        inp["cosk"][oc * 128:(oc + 1) * 128, c0:c0 + TCH], s_rope)
                ctx.dma("gpsimd", sinb[pb][:],
                        inp["sink"][oc * 128:(oc + 1) * 128, c0:c0 + TCH], s_rope)
                mm_group(B[2 + pb], lambda k, _o=oc: w_kr[:, k * D + _o * 128:k * D + _o * 128 + 128],
                         lambda k: xln_t[k], 8, TCH)
                if m1_guard[pb] is not None:
                    ctx.wait("scalar", s_ve, m1_guard[pb])
                evac(B[2 + pb], m1b[pb][:], TCH, bias=smalls["a_bkr"][:, oc:oc + 1])
                rb = B[4] if pb == 0 else B[5]
                rb.mm_wait()
                ctx.wait("tensor", s_sc)
                ctx.op("tensor", "matmul", rb.ap(128, TCH), permb[:], m1b[pb][:],
                       start=True, stop=True, inc=[(s_mm, 1)])
                mm_group(B[pb], lambda k, _o=oc: w_uk[:, k * D + _o * 128:k * D + _o * 128 + 128],
                         lambda k: ctkv[k][:, c0:c0 + TCH], 4, TCH)
                ctx.wait("vector", s_mm)
                ctx.wait("vector", s_rope)
                ctx.op("vector", "tensor_mul", tt0[:, 0:TCH], m1b[pb][:], cosb[pb][:])
                ctx.op("vector", "tensor_mul", tt1[:, 0:TCH], rb.ap(128, TCH), sinb[pb][:])
                ctx.op("vector", "tensor_add", tt0[:, 0:TCH], tt0[:, 0:TCH], tt1[:, 0:TCH])
                ctx.op("vector", "scalar_tensor_tensor",
                       out=tt1[:, 0:TCH], in0=B[pb].ap(128, TCH),
                       scalar=smalls["a_buk"][:, oc:oc + 1], in1=tt0[:, 0:TCH],
                       op0=ALU.add, op1=ALU.add)
                ctx.op("vector", "tensor_copy", out=Kp[oc][:, c0:c0 + TCH],
                       in_=tt1[:, 0:TCH], inc=[(s_ve, 1)])
                rb.set_guard(s_ve)
                B[pb].set_guard(s_ve)
                rope_guard[0] = s_ve.n
                m1_guard[pb] = s_ve.n

    if "xln" in debug:
        t = dbg_out("xln", (8, 128, TCH))
        ctx.wait("gpsimd", s_ve)
        for f in range(8):
            ctx.dma("gpsimd", t[f], xln_t[f], s_st)
    if "ctkv" in debug:
        t = dbg_out("ctkv", (4, 128, S))
        ctx.wait("gpsimd", s_sc)
        for f in range(4):
            ctx.dma("gpsimd", t[f], ctkv[f][:], s_st)
    if "ctq" in debug:
        t = dbg_out("ctq", (4, 128, NT))
        ctx.wait("gpsimd", s_sc)
        for f in range(4):
            ctx.dma("gpsimd", t[f], ctq[f][:], s_st)
    if "K" in debug:
        t = dbg_out("K", (8, 128, S))
        ctx.wait("gpsimd", s_ve)
        for p in range(8):
            ctx.dma("gpsimd", t[p], Kp[p][:], s_st)

    phA.close()

    QB = 512

    def phase_waits(engs):
        for e in engs:
            for s in (s_ve, s_sc, s_mm, s_st, s_rope):
                ctx.wait(e, s)

    phBC = ExitStack()
    attnout = [ctx.sbuf(f"ao{p}", [128, NT], BF16, stack=phBC, side="right")
               for p in range(8)]
    if max_phase >= 2:
        phB = ExitStack()
        phase_waits(("sync", "gpsimd"))
        Qall = [ctx.sbuf(f"Qp{p}", [128, NT], BF16, stack=phB, side="right")
                for p in range(8)]
        m1qb = ctx.sbuf("m1qb", [128, 256], BF16, stack=phB, side="right")
        cosq = ctx.sbuf("cosq", [128, 256], F32, stack=phB, side="right")
        sinq = ctx.sbuf("sinq", [128, 256], F32, stack=phB, side="right")
        ttB0 = ctx.sbuf("ttB0", [128, 256], F32, stack=phB, side="right")
        ttB1 = ctx.sbuf("ttB1", [128, 256], F32, stack=phB, side="right")

        phBw = ExitStack()
        w_uq = load_weight_bf16("a_wuq", inp["a_wuq"], DQ, D, stack=phBw, side="right")
        ctx.wait("tensor", s_ld)
        ctx.wait("tensor", s_ve)
        for p in range(8):
            for s8 in range(8):
                q0 = s8 * 256
                mm_group(B[s8 % 2], lambda k, _p=p: w_uq[:, k * D + _p * 128:k * D + _p * 128 + 128],
                         lambda k, _q=q0: ctq[k][:, _q:_q + 256], 4, 256)
                evac(B[s8 % 2], Qall[p][:, q0:q0 + 256], 256,
                     bias=smalls["a_buq"][:, p:p + 1])
        phBw.close()
        phBw = ExitStack()
        phase_waits(("sync",))
        w_qr = load_weight_bf16("a_wqr", inp["a_wqr"], DQ, D, stack=phBw, side="right")
        ctx.wait("tensor", s_ld)
        ctx.wait("tensor", s_ve)
        q_guards = [None]
        for p in range(8):
            for s8 in range(8):
                q0 = s8 * 256
                if q_guards[0] is not None:
                    ctx.wait("gpsimd", s_ve, q_guards[0])
                ctx.dma("gpsimd", cosq[:], inp["cosk"][p * 128:(p + 1) * 128, q0:q0 + 256], s_rope)
                ctx.dma("gpsimd", sinq[:], inp["sink"][p * 128:(p + 1) * 128, q0:q0 + 256], s_rope)
                qb0 = B[2 * (s8 % 2)]
                qb1 = B[2 * (s8 % 2) + 1]
                mm_group(qb0, lambda k, _p=p: w_qr[:, k * D + _p * 128:k * D + _p * 128 + 128],
                         lambda k, _q=q0: ctq[k][:, _q:_q + 256], 4, 256)
                if q_guards[0] is not None:
                    ctx.wait("scalar", s_ve, q_guards[0])
                evac(qb0, m1qb[:], 256, bias=smalls["a_bqr"][:, p:p + 1])
                qb1.mm_wait()
                ctx.wait("tensor", s_sc)
                ctx.op("tensor", "matmul", qb1.ap(128, 256), permb[:], m1qb[:],
                       start=True, stop=True, inc=[(s_mm, 1)])
                ctx.wait("vector", s_mm)
                ctx.wait("vector", s_rope)
                ctx.op("vector", "tensor_mul", ttB0[:], m1qb[:], cosq[:])
                ctx.op("vector", "tensor_mul", ttB1[:], qb1.ap(128, 256), sinq[:])
                ctx.op("vector", "tensor_add", ttB0[:], ttB0[:], ttB1[:])
                ctx.op("vector", "tensor_add", out=Qall[p][:, q0:q0 + 256],
                       in0=ttB0[:], in1=Qall[p][:, q0:q0 + 256], inc=[(s_ve, 1)])
                qb1.set_guard(s_ve)
                q_guards[0] = s_ve.n
        phBw.close()
        phQ.close()
        phase_waits(("sync",))
        w_uv = load_weight_bf16("a_wuv", inp["a_wuv"], DKV, D, stack=phB, side="right")
        buvr = load_small("a_buv", phB)
        identf = ctx.sbuf("identf", [128, 128], F32, stack=phB, side="right")
        ctx.dma("sync", identf[:], inp["ident"][:], s_ld)
        V2 = ctx.sbuf("V2", [128, 2 * NT], BF16, stack=phB, side="right")
        Ebuf = ctx.sbuf("Ebuf", [128, 3 * 1024], BF16, stack=phB, side="right")
        scrB = ctx.sbuf("scrB", [128, 512], F32, stack=phB, side="right")
        ava = ctx.sbuf("ava", [128, 512], F32, stack=phB, side="right")
        normc = ctx.sbuf("normc", [128, 34], F32, stack=phB, side="right")
        rowsB = ctx.sbuf("rowsB", [1, 1024], F32, stack=phB, side="right")
        cells = ctx.sbuf("cells", [1, 72], F32, stack=phB, side="right")
        ebias = ctx.sbuf("ebias", [128, 8], F32, stack=phB, side="right")

        ctx.wait("tensor", s_ld)
        SUBN = max_phase >= 3
        def norms_max(src_tile, ncc, cell_off, p):
            for j in (0, 1):
                nb = B[6]
                nb.mm_wait()
                cnt = 0
                for cc in range(ncc):
                    if cc > 0:
                        ctx.wait("scalar", s_mm)
                    ctx.op("scalar", "activation", scrB[:], src_tile[:, cc * 512:(cc + 1) * 512],
                           AF.Square, inc=[(s_sc, 1)])
                    ctx.wait("tensor", s_sc)
                    for c4 in range(4):
                        ctx.op("tensor", "matmul",
                               nb.t[0:128, nb.base + cnt:nb.base + cnt + 1],
                               scrB[j * 64:(j + 1) * 64, c4 * 128:(c4 + 1) * 128],
                               ones128[j * 64:(j + 1) * 64, :],
                               start=(cnt == 0), stop=(cnt == ncc * 4 - 1),
                               inc=([(s_mm, 1)] if c4 == 3 else None))
                        cnt += 1
                ctx.wait("scalar", s_mm)
                ctx.op("scalar", "copy", normc[:, 0:cnt], nb.t[0:128, nb.base:nb.base + cnt],
                       inc=[(s_sc, 1)])
                nb.set_guard(s_sc)
                ctx.wait("vector", s_sc)
                ctx.op("vector", "reduce_max", normc[:, 32:33], normc[:, 0:cnt], axis=AX.X,
                       inc=[(s_ve, 1)])
                B[7].mm_wait()
                ctx.wait("tensor", s_ve)
                ctx.op("tensor", "transpose", B[7].ap(1, 128), normc[:, 32:33], identf[:],
                       inc=[(s_mm, 1)])
                ctx.wait("scalar", s_mm)
                ctx.op("scalar", "copy", rowsB[:, 0:128], B[7].ap(1, 128), inc=[(s_sc, 1)])
                B[7].set_guard(s_sc)
                ctx.wait("vector", s_sc)
                ctx.op("vector", "reduce_max", cells[:, cell_off + 2 * p + j:cell_off + 2 * p + j + 1],
                       rowsB[:, 0:128], axis=AX.X, inc=[(s_ve, 1)])

        for p in range(8):
            if not SUBN:
                break
            norms_max(Kp[p], 8, 0, p)
            norms_max(Qall[p], 4, 16, p)

        for p in range(8 if SUBN else 0):
            c4 = 32 + 4 * p
            ctx.wait("vector", s_ve)
            ctx.op("vector", "tensor_max", cells[:, c4:c4 + 1], cells[:, 2 * p:2 * p + 1],
                   cells[:, 2 * p + 1:2 * p + 2])
            ctx.op("vector", "tensor_max", cells[:, c4 + 1:c4 + 2],
                   cells[:, 16 + 2 * p:16 + 2 * p + 1],
                   cells[:, 17 + 2 * p:18 + 2 * p], inc=[(s_ve, 1)])
            ctx.wait("scalar", s_ve)
            ctx.op("scalar", "mul", cells[:, c4 + 2:c4 + 3], cells[:, c4:c4 + 1],
                   cells[:, c4 + 1:c4 + 2])
            ctx.op("scalar", "activation", cells[:, c4 + 3:c4 + 4],
                   cells[:, c4 + 2:c4 + 3], AF.Sqrt, inc=[(s_sc, 1)])
            B[7].mm_wait()
            ctx.wait("tensor", s_sc)
            ctx.op("tensor", "matmul", B[7].ap(128, 1), ones_col[:],
                   cells[:, c4 + 3:c4 + 4],
                   start=True, stop=True, inc=[(s_mm, 1)])
            ctx.wait("scalar", s_mm)
            ctx.op("scalar", "activation", ebias[:, p:p + 1], B[7].ap(128, 1), AF.Identity,
                   scale=-1.0 / SCALE, inc=[(s_sc, 1)])
            B[7].set_guard(s_sc)

        if "enorm" in debug:
            t = dbg_out("enorm", (1, 40))
            t2 = dbg_out("ebias", (128, 8))
            ctx.wait("gpsimd", s_sc)
            ctx.wait("gpsimd", s_ve)
            ctx.dma("gpsimd", t[:], cells[:], s_st)
            ctx.dma("gpsimd", t2[:], ebias[:], s_st)
        NKC2 = S // 128
        ebuf_guard = [None] * 2
        for p in range(8 if max_phase >= 4 else 0):
            for j in (0, 1):
                h = 2 * p + j
                for g8 in range(8):
                    vb = B[6 + (j * 8 + g8) % 2]
                    vb.mm_wait()
                    for i4 in range(4):
                        kt = g8 * 4 + i4
                        for c in range(4):
                            ctx.op("tensor", "matmul", vb.ap(128, 64, off=i4 * 64),
                                   ctkv[c][:, kt * 128:(kt + 1) * 128],
                                   w_uv[:, c * D + h * 64:c * D + h * 64 + 64],
                                   start=(c == 0 and i4 == 0), stop=False)
                        ctx.op("tensor", "matmul", vb.ap(128, 64, off=i4 * 64),
                               ones_col[:], buvr[:, h * 64:h * 64 + 64],
                               start=False, stop=(i4 == 3),
                               inc=([(s_mm, 1)] if i4 == 3 else None))
                    ctx.wait("scalar", s_mm)
                    ctx.op("scalar", "copy", V2[:, j * NT + g8 * 256:j * NT + (g8 + 1) * 256],
                           vb.ap(128, 256), inc=[(s_sc, 1)])
                    vb.set_guard(s_sc)
            for qi in range(NT // QB if SUB != "v" else 0):
                q0 = qi * QB
                avb = B[6]
                avb.mm_wait()
                dn0 = B[7]
                dn0.mm_wait()
                B[4].mm_wait()
                av_started = False
                pend = [None, None]

                def issue_S(g):
                    reg = REG[g % 2]
                    rb0, rb1 = B[2 * (g % 2)], B[2 * (g % 2) + 1]
                    rb0.mm_wait()
                    rb1.mm_wait()
                    for j in (0, 1):
                        ctx.op("tensor", "matmul",
                               reg[0:128, j * 512:(j + 1) * 512],
                               Kp[p][j * 64:(j + 1) * 64, g * 128:(g + 1) * 128],
                               Qall[p][j * 64:(j + 1) * 64, q0:q0 + QB],
                               start=True, stop=True,
                               inc=([(s_mm, 1)] if j == 1 else None))
                    pend[g % 2] = (rb0, rb1, s_mm.n)

                issue_S(0)
                for g in range(NKC2):
                    kc = g
                    reg = REG[g % 2]
                    rb0, rb1, s_after = pend[g % 2]
                    if g + 1 < NKC2:
                        issue_S(g + 1)
                    eb = Ebuf[:, (g % 2) * 1024:(g % 2 + 1) * 1024]
                    ctx.wait("scalar", s_mm, s_after)
                    ctx.op("scalar", "activation", eb, reg[0:128, 0:1024], AF.Exp,
                           bias=ebias[:, p:p + 1], scale=1.0 / SCALE, inc=[(s_sc, 1)])
                    rb0.set_guard(s_sc)
                    rb1.set_guard(s_sc)
                    last = (g == NKC2 - 1)
                    ctx.wait("tensor", s_sc)
                    if SUB not in ("s", "s0"):
                        for j in (0, 1):
                            esl = eb[:, j * 512:(j + 1) * 512]
                            ctx.op("tensor", "matmul", avb.ap(64, QB, p0=j * 64),
                                   V2[:, j * NT + kc * 64:j * NT + (kc + 1) * 64],
                                   esl, start=(not av_started),
                                   stop=last,
                                   tile_position=(0, 64 * j),
                                   skip_group_check=(j == 1))
                            dap = dn0.ap(1, QB) if j == 0 else B[4].ap(1, QB)
                            ctx.op("tensor", "matmul", dap,
                                   ones128b[:], esl,
                                   start=(not av_started),
                                   stop=last,
                                   inc=([(s_mm, 1)] if (last and j == 1) else None))
                        av_started = True
                if SUB in ("s", "s0", "av"):
                    continue
                ctx.wait("vector", s_mm)
                ctx.op("vector", "reciprocal", rowsB[:, 0:QB], dn0.ap(1, QB))
                ctx.op("vector", "reciprocal", rowsB[:, QB:2 * QB], B[4].ap(1, QB),
                       inc=[(s_ve, 1)])
                dn0.set_guard(s_ve)
                B[4].set_guard(s_ve)
                ctx.wait("scalar", s_mm)
                ctx.op("scalar", "copy", ava[:], avb.ap(128, QB), inc=[(s_sc, 1)])
                avb.set_guard(s_sc)
                bcb = B[5]
                bcb.mm_wait()
                ctx.wait("tensor", s_ve)
                ctx.op("tensor", "matmul", bcb.ap(64, QB), ones_col[:, 0:64],
                       rowsB[:, 0:QB], start=True, stop=True, tile_position=(0, 0))
                ctx.op("tensor", "matmul", bcb.ap(64, QB, p0=64), ones_col[:, 0:64],
                       rowsB[:, QB:2 * QB], start=True, stop=True, tile_position=(0, 64),
                       skip_group_check=True, inc=[(s_mm, 1)])
                ctx.wait("vector", s_mm)
                ctx.wait("vector", s_sc)
                ctx.op("vector", "tensor_mul", attnout[p][:, q0:q0 + QB], ava[:],
                       bcb.ap(128, QB), inc=[(s_ve, 1)])
                bcb.set_guard(s_ve)

        if "attnout" in debug:
            t = dbg_out("attnout", (8, 128, NT))
            ctx.wait("gpsimd", s_ve)
            for p in range(8):
                ctx.dma("gpsimd", t[p], attnout[p][:], s_st)
        phB.close()
    phAB.close()

    r_t = [ctx.sbuf(f"r{f}", [128, NT], F32) for f in range(8)]
    if max_phase >= 5:
        phC = ExitStack()
        phase_waits(("sync", "gpsimd"))
        w_o = load_weight_bf16("a_wo", inp["a_wo"], D, D, stack=phC, side="right")
        xob = ctx.sbuf("xob", [128, 2 * 512], F32, stack=phC, side="right")
        scrC = [ctx.sbuf(f"scrC{i}", [128, 512], F32, stack=phC, side="right")
                for i in range(2)]
        xo_guard = [None, None]
        ctx.wait("tensor", s_ld)
        for oc in range(8):
            for s4 in range(4):
                q0 = s4 * 512
                pb = s4 % 2
                if xo_guard[pb] is not None:
                    ctx.wait("gpsimd", s_ve, xo_guard[pb])
                ctx.dma("gpsimd", xob[:, pb * 512:(pb + 1) * 512],
                        inp["x_own"][oc * 128:(oc + 1) * 128, q0:q0 + 512], s_rope)
                ctx.wait("tensor", s_ve)
                mm_group(B[oc % 2], lambda k, _o=oc: w_o[:, k * D + _o * 128:k * D + _o * 128 + 128],
                         lambda k, _q=q0: attnout[k][:, _q:_q + 512], 8, 512)
                ctx.wait("scalar", s_mm)
                ctx.wait("scalar", s_ve)
                ctx.op("scalar", "activation", scrC[pb][:], B[oc % 2].ap(128, 512), AF.Identity,
                       bias=smalls["a_bo"][:, oc:oc + 1], inc=[(s_sc, 1)])
                B[oc % 2].set_guard(s_sc)
                ctx.wait("vector", s_sc)
                ctx.wait("vector", s_rope)
                ctx.op("vector", "tensor_add", r_t[oc][:, q0:q0 + 512], scrC[pb][:],
                       xob[:, pb * 512:(pb + 1) * 512], inc=[(s_ve, 1)])
                xo_guard[pb] = s_ve.n
        phC.close()
        phBC.close()

    if "r1" in debug:
        t = dbg_out("r1", (8, 128, NT))
        ctx.wait("gpsimd", s_ve)
        for f in range(8):
            ctx.dma("gpsimd", t[f], r_t[f][:], s_st)

    ln_uid = [0]

    def make_ln_scratch(stack):
        ln_uid[0] += 1
        u = ln_uid[0]
        sc = {}
        for nm, shape in (("sq", [128, 512]), ("mu_b", [128, 512]),
                          ("rs_b", [128, 512]), ("t0", [128, 512]),
                          ("murow", [1, 512]), ("rsrow", [1, 512]),
                          ("va", [1, 512]), ("t1", [1, 512]),
                          ("mu2", [1, 512])):
            sc[nm] = ctx.sbuf(f"ln{u}_{nm}", shape, F32, stack=stack, side="right")
        return sc

    def emit_ln(srcs, T, outs, sc, sA, sB, bC, gamma=None, beta=None):
        mu_row = sc["murow"][:, 0:T]
        rs_row = sc["rsrow"][:, 0:T]
        va_row = sc["va"][:, 0:T]
        t1_row = sc["t1"][:, 0:T]
        mu2_row = sc["mu2"][:, 0:T]
        ctx.wait("tensor", s_ve)
        ctx.wait("tensor", s_sc)
        ctx.wait("scalar", s_ve)
        sA.mm_wait()
        for f in range(8):
            ctx.op("tensor", "matmul", sA.ap(1, T), ones128[:], srcs[f],
                   start=(f == 0), stop=(f == 7), inc=([(s_mm, 1)] if f == 7 else None))
        sB.mm_wait()
        for f in range(8):
            if sq_guard:
                ctx.wait("scalar", s_mm, val=sq_guard[-1])
            ctx.op("scalar", "activation", sc["sq"][:, 0:T], srcs[f], AF.Square,
                   inc=[(s_sc, 1)])
            ctx.wait("tensor", s_sc)
            ctx.op("tensor", "matmul", sB.ap(1, T), ones128[:], sc["sq"][:, 0:T],
                   start=(f == 0), stop=(f == 7), inc=[(s_mm, 1)])
            sq_guard.append(s_mm.n)
        ctx.wait("vector", s_mm)
        ctx.op("vector", "tensor_scalar_mul", mu_row, sA.ap(1, T), 1.0 / D)
        ctx.op("vector", "tensor_scalar_mul", t1_row, sB.ap(1, T), 1.0 / D,
               inc=[(s_ve, 1)])
        sA.set_guard(s_ve)
        sB.set_guard(s_ve)
        ctx.op("vector", "tensor_mul", va_row, mu_row, mu_row)
        ctx.op("vector", "tensor_sub", va_row, t1_row, va_row)
        ctx.op("vector", "tensor_scalar_add", va_row, va_row, EPS, inc=[(s_ve, 1)])
        ctx.wait("scalar", s_ve)
        ctx.op("scalar", "activation", rs_row, va_row, AF.Sqrt, inc=[(s_sc, 1)])
        ctx.wait("vector", s_sc)
        ctx.op("vector", "reciprocal", t1_row, rs_row, inc=[(s_ve, 1)])
        ctx.wait("scalar", s_ve)
        ctx.op("scalar", "square", mu2_row, t1_row, inc=[(s_sc, 1)])
        ctx.wait("vector", s_sc)
        ctx.op("vector", "tensor_mul", mu2_row, mu2_row, va_row)
        ctx.op("vector", "tensor_scalar", mu2_row, mu2_row, -0.5, 1.5,
               op0=ALU.mult, op1=ALU.add)
        ctx.op("vector", "tensor_mul", rs_row, mu2_row, t1_row, inc=[(s_ve, 1)])
        bC.mm_wait()
        sA.mm_wait()
        ctx.wait("tensor", s_ve)
        ctx.op("tensor", "matmul", bC.ap(128, T, 0), ones_col[:], mu_row,
               start=True, stop=True)
        ctx.op("tensor", "matmul", sA.ap(128, T, 0), ones_col[:], rs_row,
               start=True, stop=True, inc=[(s_mm, 1)])
        ctx.wait("scalar", s_mm)
        ctx.wait("scalar", s_ve)
        ctx.op("scalar", "copy", sc["mu_b"][:, 0:T], bC.ap(128, T, 0))
        ctx.op("scalar", "copy", sc["rs_b"][:, 0:T], sA.ap(128, T, 0), inc=[(s_sc, 1)])
        bC.set_guard(s_sc)
        sA.set_guard(s_sc)
        ctx.wait("vector", s_sc)
        for f in range(8):
            ctx.op("vector", "tensor_sub", sc["t0"][:, 0:T], srcs[f], sc["mu_b"][:, 0:T])
            if gamma is None:
                ctx.op("vector", "tensor_mul", outs[f], sc["t0"][:, 0:T],
                       sc["rs_b"][:, 0:T], inc=[(s_ve, 1)])
            else:
                ctx.op("vector", "tensor_mul", sc["t0"][:, 0:T], sc["t0"][:, 0:T],
                       sc["rs_b"][:, 0:T])
                ctx.op("vector", "tensor_scalar", outs[f], sc["t0"][:, 0:T],
                       gamma[:, f:f + 1], beta[:, f:f + 1],
                       op0=ALU.mult, op1=ALU.add, inc=[(s_ve, 1)])

    class WStream:

        _uid = [0]

        def __init__(self, dram, ncols, stack, stg_cols=None, nbuf=2):
            WStream._uid[0] += 1
            u = WStream._uid[0]
            self.dram = dram
            self.ncols = ncols
            self.nbuf = nbuf
            self.stg_cols = min(ncols, stg_cols or ncols)
            self.piece = min(2048, self.stg_cols)
            self.nslots = max(1, self.stg_cols // self.piece)
            self.stg = ctx.sbuf(f"wstg{u}", [128, self.stg_cols], F32,
                                stack=stack, side="right")
            self.wb = [ctx.sbuf(f"wb{u}_{i}", [128, ncols], BF16, stack=stack, side="right")
                       for i in range(nbuf)]
            self.stg_guard = [None] * max(1, self.nslots)
            self.wb_guard = [None] * nbuf
            self.ready = [0] * nbuf
            self.born = False

        def load(self, oc, buf, width=None):
            if not self.born:
                for s in (s_mm, s_ve, s_sc):
                    ctx.wait("sync", s)
                self.born = True
            width = self.ncols if width is None else width
            for i, c0 in enumerate(range(0, width, self.piece)):
                cw = min(self.piece, width - c0)
                sl = (i % self.nslots) * self.piece
                sw = min(self.piece, self.stg_cols - sl)
                assert cw <= sw
                if self.stg_guard[i % self.nslots] is not None:
                    ctx.wait("sync", s_ve, self.stg_guard[i % self.nslots])
                ctx.dma("sync", self.stg[:, sl:sl + cw], self.dram[oc][:, c0:c0 + cw],
                        s_ld)
                if self.wb_guard[buf] is not None:
                    ctx.wait("vector", s_mm, self.wb_guard[buf])
                ctx.wait("vector", s_ld)
                ctx.op("vector", "tensor_copy", out=self.wb[buf][:, c0:c0 + cw],
                       in_=self.stg[:, sl:sl + cw], inc=[(s_ve, 1)])
                self.stg_guard[i % self.nslots] = s_ve.n
            self.ready[buf] = s_ve.n

        def used(self, buf):
            self.wb_guard[buf] = s_mm.n

    TQ = 512
    if max_phase >= 6:
        phD = ExitStack()
        phase_waits(("sync", "gpsimd", "scalar"))
        lnsc = make_ln_scratch(phD)
        xq = ctx.sbuf("xqD", [128, 8 * TQ], BF16, stack=phD, side="right")
        xq_t = [xq[:, f * TQ:(f + 1) * TQ] for f in range(8)]
        hq = ctx.sbuf("hqD", [128, 32 * TQ], BF16, stack=phD, side="right")
        g1q = ctx.sbuf("g1qD", [128, 32 * TQ], BF16, stack=phD, side="right")
        tD0 = lnsc["mu_b"]
        tD1 = lnsc["rs_b"]
        wsh = WStream(inp["f_h1x"], DH, phD)
        ws12 = wsh

        for q in range(NT // TQ):
            q0 = q * TQ
            emit_ln([r_t[f][:, q0:q0 + TQ] for f in range(8)], TQ, xq_t,
                    lnsc, B[4], B[5], B[3])
            ws12.dram = inp["f_w12x"]
            ws12.load(0, 0, width=2048)
            ctx.wait("tensor", s_ve)
            ctx.wait("tensor", s_sc)
            for oc in range(32):
                bu = oc % 2
                if oc < 31:
                    ws12.load(oc + 1, 1 - bu, width=2048)
                wb = ws12.wb[bu]
                ctx.wait("tensor", s_ve, ws12.ready[bu])
                mm_group(B[2 * bu], lambda k, _w=wb: _w[:, k * 128:(k + 1) * 128],
                         lambda k: xq_t[k], 8, TQ)
                mm_group(B[2 * bu + 1], lambda k, _w=wb: _w[:, 1024 + k * 128:1024 + (k + 1) * 128],
                         lambda k: xq_t[k], 8, TQ)
                ws12.used(bu)
                ctx.wait("scalar", s_mm)
                ctx.wait("scalar", s_ve)
                ctx.op("scalar", "activation", tD0[:], B[2 * bu].ap(128, TQ), AF.Silu,
                       scale=smalls["f_s1"][:, oc:oc + 1])
                ctx.op("scalar", "activation", tD1[:], B[2 * bu + 1].ap(128, TQ),
                       AF.Identity, scale=smalls["f_s2"][:, oc:oc + 1], inc=[(s_sc, 1)])
                B[2 * bu].set_guard(s_sc)
                B[2 * bu + 1].set_guard(s_sc)
                ctx.wait("vector", s_sc)
                ctx.op("vector", "tensor_mul", hq[:, oc * TQ:(oc + 1) * TQ],
                       tD0[:], tD1[:], inc=[(s_ve, 1)])
            for stage, dram, scl, src_q, dst in (
                    (2, inp["f_h1x"], "f_sh1", hq, g1q),
                    (3, inp["f_h2x"], "f_sh2", g1q, hq)):
                wsh.dram = dram
                wsh.load(0, 0)
                ctx.wait("tensor", s_ve)
                ctx.wait("tensor", s_sc)
                for oc in range(32):
                    bu = oc % 2
                    if oc < 31:
                        wsh.load(oc + 1, 1 - bu)
                    wb = wsh.wb[bu]
                    ctx.wait("tensor", s_ve, wsh.ready[bu])
                    mm_group(B[2 * bu], lambda k, _w=wb: _w[:, k * 128:(k + 1) * 128],
                             lambda k, _s=src_q: _s[:, k * TQ:k * TQ + TQ], 32, TQ)
                    wsh.used(bu)
                    ctx.wait("scalar", s_mm)
                    ctx.op("scalar", "activation", dst[:, oc * TQ:(oc + 1) * TQ],
                           B[2 * bu].ap(128, TQ), AF.Gelu,
                           scale=smalls[scl][:, oc:oc + 1], inc=[(s_sc, 1)])
                    B[2 * bu].set_guard(s_sc)
            wsh.dram = inp["f_w3x"]
            wsh.load(0, 0)
            ctx.wait("tensor", s_sc)
            for oc in range(8):
                bu = oc % 2
                if oc < 7:
                    wsh.load(oc + 1, 1 - bu)
                wb = wsh.wb[bu]
                ctx.wait("tensor", s_ve, wsh.ready[bu])
                mm_group(B[2 * bu], lambda k, _w=wb: _w[:, k * 128:(k + 1) * 128],
                         lambda k: hq[:, k * TQ:k * TQ + TQ], 32, TQ)
                wsh.used(bu)
                ctx.wait("scalar", s_mm)
                ctx.wait("scalar", s_ve)
                ctx.op("scalar", "activation", tD0[:] if bu == 0 else tD1[:],
                       B[2 * bu].ap(128, TQ), AF.Identity,
                       scale=smalls["f_s3"][:, oc:oc + 1], inc=[(s_sc, 1)])
                B[2 * bu].set_guard(s_sc)
                ctx.wait("vector", s_sc)
                ctx.op("vector", "tensor_add", r_t[oc][:, q0:q0 + TQ],
                       tD0[:] if bu == 0 else tD1[:], r_t[oc][:, q0:q0 + TQ],
                       inc=[(s_ve, 1)])
        phD.close()

    if "r2" in debug:
        t = dbg_out("r2", (8, 128, NT))
        ctx.wait("gpsimd", s_ve)
        for f in range(8):
            ctx.dma("gpsimd", t[f], r_t[f][:], s_st)

    s_cc = ctx.sem("s_cc")
    cc_uid = [0]
    TH = NT // 2

    def allgather_prev(src_tile, kcols, recv_tile, stack):
        nc = ctx.nc
        cc_uid[0] += 1
        u = cc_uid[0]
        ncols = 128 * kcols
        cin = nc.dram_tensor(f"ccin{u}", [1, ncols], F32).ap()
        cout = nc.dram_tensor(f"ccout{u}", [8, ncols], F32).ap()
        ags = ctx.sbuf(f"ags{u}", [8, ncols], F32, stack=stack, side="right")
        rowt = ctx.sbuf(f"ccrow{u}", [1, ncols], F32, stack=stack, side="right")
        ctx.wait("gpsimd", s_ve)
        ctx.wait("gpsimd", s_sc)
        ctx.dma("gpsimd", cin.rearrange("o (p c) -> (o p) c", p=128), src_tile, s_st)
        ctx.wait("gpsimd", s_st)
        ctx.op("gpsimd", "collective_compute", "AllGather", ALU.bypass,
               replica_groups=[list(range(8))], ins=[cin[:]], outs=[cout[:]],
               inc=[(s_cc, 1)])
        ctx.wait("gpsimd", s_cc)
        ctx.dma("gpsimd", ags[:], cout[:], s_gd)
        ctx.wait("tensor", s_gd)
        ctx.wait("tensor", s_ld)
        rb = B[0]
        rb.mm_wait()
        B[1].mm_wait()
        npieces = (ncols + 1023) // 1024
        for pz in range(npieces):
            p0 = pz * 1024
            pw = min(1024, ncols - p0)
            for i in range(0, pw, 512):
                cw = min(512, pw - i)
                ctx.op("tensor", "matmul", REG[0][0:1, i:i + cw],
                       smalls["mask8"][:], ags[:, p0 + i:p0 + i + cw],
                       start=True, stop=True,
                       inc=([(s_mm, 1)] if i + cw == pw else None))
            ctx.wait("scalar", s_mm)
            ctx.op("scalar", "copy", rowt[:, p0:p0 + pw], REG[0][0:1, 0:pw],
                   inc=[(s_sc, 1)])
            rb.set_guard(s_sc)
            B[1].set_guard(s_sc)
        ctx.wait("gpsimd", s_sc)
        ctx.dma("gpsimd", recv_tile, rowt[:], s_gd)
        return s_gd.n

    if max_phase >= 7:
        phE = ExitStack()
        phase_waits(("sync", "gpsimd", "scalar"))
        lnsc = make_ln_scratch(phE)
        NQ = 4
        TQE = 512
        wrow = ctx.sbuf("wrow", [1, D], F32, stack=phE, side="right")
        drow = load_small("r_decay_row", phE)
        ccol = ctx.sbuf("ccolE", [128, 32], F32, stack=phE, side="right")
        w_col = ccol[:, 0:8]
        ew_col = ccol[:, 8:16]
        eu_col = ccol[:, 16:24]
        ewq_col = ccol[:, 24:32]
        ctx.wait("scalar", s_ld)
        ctx.op("scalar", "activation", wrow[:], drow[:], AF.Exp)
        ctx.op("scalar", "activation", wrow[:], wrow[:], AF.Identity, scale=-1.0)
        ctx.op("scalar", "activation", w_col, smalls["r_decay"][:], AF.Exp)
        ctx.op("scalar", "activation", w_col, w_col, AF.Identity, scale=-1.0)
        ctx.op("scalar", "activation", ew_col, w_col, AF.Exp)
        ctx.op("scalar", "activation", eu_col, smalls["r_first"][:], AF.Exp)
        ctx.op("scalar", "activation", ewq_col, w_col, AF.Exp, scale=float(TQE),
               inc=[(s_sc, 1)])
        iotar = load_small("iota_rev", phE)
        recvh = ctx.sbuf("recvh", [128, 8], F32, stack=phE, side="right")
        recvAB = ctx.sbuf("recvAB", [128, 16], F32, stack=phE, side="right")
        hlast_c = ctx.sbuf("hlast_c", [128, 8], F32, stack=phE, side="right")
        dots = ctx.sbuf("dotsE", [128, 64], F32, stack=phE, side="right")
        dA = [dots[:, q * 16:q * 16 + 8] for q in range(4)]
        dB = [dots[:, q * 16 + 8:q * 16 + 16] for q in range(4)]
        sendAB = ctx.sbuf("sendAB", [128, 16], F32, stack=phE, side="right")
        ABfin = ctx.sbuf("ABfin", [128, 16], F32, stack=phE, side="right")
        Afin = ABfin[:, 0:8]
        Bfin = ABfin[:, 8:16]
        g_last = ctx.sbuf("g_lastE", [128, 8], F32, stack=phE, side="right")
        g0b = [ctx.sbuf(f"g0b{f}", [128, 513], BF16, stack=phE, side="right")
               for f in range(8)]
        ccscr = ctx.sbuf("ccscr", [8, 2048], F32, stack=phE, side="right")

        def allgather_prev(src_tile, kcols, recv_tile):
            nc = ctx.nc
            cc_uid[0] += 1
            u = cc_uid[0]
            ncols = 128 * kcols
            cin = nc.dram_tensor(f"ccin{u}", [1, ncols], F32).ap()
            cout = nc.dram_tensor(f"ccout{u}", [8, ncols], F32).ap()
            ctx.wait("gpsimd", s_ve)
            ctx.wait("gpsimd", s_sc)
            ctx.dma("gpsimd", cin.rearrange("o (p c) -> (o p) c", p=128), src_tile, s_st)
            ctx.wait("gpsimd", s_st)
            ctx.op("gpsimd", "collective_compute", "AllGather", ALU.bypass,
                   replica_groups=[list(range(8))], ins=[cin[:]], outs=[cout[:]],
                   inc=[(s_cc, 1)])
            ctx.wait("gpsimd", s_cc)
            ctx.wait("gpsimd", s_sc)
            ctx.dma("gpsimd", ccscr[:, 0:ncols], cout[:], s_gd)
            if u == 1 and "eccscr" in debug:
                tdbg = dbg_out("eccscr", (8, 2048))
                ctx.wait("gpsimd", s_gd)
                ctx.dma("gpsimd", tdbg[:, 0:ncols], ccscr[:, 0:ncols], s_st)
            ctx.wait("tensor", s_gd)
            ctx.wait("tensor", s_ld)
            rb = B[0]
            for pz in range((ncols + 1023) // 1024):
                rb.mm_wait()
                B[1].mm_wait()
                p0 = pz * 1024
                pw = min(1024, ncols - p0)
                for i in range(0, pw, 512):
                    cw = min(512, pw - i)
                    ctx.op("tensor", "matmul", REG[0][0:1, i:i + cw],
                           smalls["mask8"][:], ccscr[0:8, p0 + i:p0 + i + cw],
                           start=True, stop=True,
                           inc=([(s_mm, 1)] if i + cw == pw else None))
                ctx.wait("scalar", s_mm)
                ctx.op("scalar", "copy", ccscr[0:1, p0:p0 + pw], REG[0][0:1, 0:pw],
                       inc=[(s_sc, 1)])
                rb.set_guard(s_sc)
                B[1].set_guard(s_sc)
            ctx.wait("gpsimd", s_sc)
            ctx.dma("gpsimd", recv_tile, ccscr[0:1, 0:ncols], s_gd)
            return s_gd.n

        def emit_h_quarter(q, tag, hb, h0t, want_xr, xkb, xvb, xrb):
            c0 = q * TQE
            emit_ln([r_t[f][:, c0:c0 + TQE] for f in range(8)], TQE,
                    [h0t[:, f * TQE:(f + 1) * TQE] for f in range(8)],
                    lnsc, B[4], B[5], B[3])
            emit_ln([h0t[:, f * TQE:(f + 1) * TQE] for f in range(8)], TQE,
                    [hb[f][:, 1:1 + TQE] for f in range(8)],
                    lnsc, B[4], B[5], B[3],
                    gamma=smalls["r_ln1w"], beta=smalls["r_ln1b"])
            if q == 0:
                ctx.wait("vector", s_gd, c1_done[0])
            for f in range(8):
                src = recvh[:, f:f + 1] if q == 0 else hlast_c[:, f:f + 1]
                ctx.op("vector", "tensor_copy", out=hb[f][:, 0:1], in_=src)
            for f in range(8):
                ctx.op("vector", "tensor_copy", out=hlast_c[:, f:f + 1],
                       in_=hb[f][:, TQE:TQE + 1])
            mixes = [(xkb, "r_mixk"), (xvb, "r_mixv")] +                     ([(xrb, "r_mixr")] if want_xr else [])
            for f in range(8):
                ctx.op("vector", "tensor_sub", lnsc["t0"][:], hb[f][:, 1:513],
                       hb[f][:, 0:512])
                for i, (dst, mixn) in enumerate(mixes):
                    ctx.op("vector", "scalar_tensor_tensor", out=dst[f][:],
                           in0=lnsc["t0"][:], scalar=smalls[mixn][:, f:f + 1],
                           in1=hb[f][:, 0:512], op0=ALU.mult, op1=ALU.add,
                           inc=([(s_ve, 1)] if (f == 7 and i == len(mixes) - 1) else None))

        def emit_ffn_slice(c0, gbuf, stack):
            gkb = [ctx.sbuf(f"gkbF{c0}_{f}", [128, 512], BF16, stack=stack, side="right")
                   for f in range(8)]
            grb = [ctx.sbuf(f"grbF{c0}_{f}", [128, 512], BF16, stack=stack, side="right")
                   for f in range(8)]
            kkb = ctx.sbuf(f"kkbF{c0}", [128, 32 * 512], BF16, stack=stack, side="right")
            ctx.wait("vector", s_ve)
            for f in range(8):
                ctx.op("vector", "tensor_sub", lnsc["t0"][:, 0:512],
                       gbuf[f][:, 1:513], gbuf[f][:, 0:512])
                ctx.op("vector", "scalar_tensor_tensor", out=gkb[f][:],
                       in0=lnsc["t0"][:, 0:512], scalar=smalls["r_fmixk"][:, f:f + 1],
                       in1=gbuf[f][:, 0:512], op0=ALU.mult, op1=ALU.add)
                ctx.op("vector", "scalar_tensor_tensor", out=grb[f][:],
                       in0=lnsc["t0"][:, 0:512], scalar=smalls["r_fmixr"][:, f:f + 1],
                       in1=gbuf[f][:, 0:512], op0=ALU.mult, op1=ALU.add,
                       inc=([(s_ve, 1)] if f == 7 else None))
            gk_rdy = s_ve.n
            wsK = WStream(inp["r_fwkx"], D, stack)
            wsK.load(0, 0)
            for oc in range(32):
                bu = oc % 2
                if oc < 31:
                    wsK.load(oc + 1, 1 - bu)
                wb = wsK.wb[bu]
                ctx.wait("tensor", s_ve, wsK.ready[bu])
                ctx.wait("tensor", s_ve, gk_rdy)
                mm_group(B[2 * bu], lambda k, _w=wb: _w[:, k * 128:(k + 1) * 128],
                         lambda k: gkb[k][:], 8, 512)
                wsK.used(bu)
                ctx.wait("scalar", s_mm)
                ctx.wait("scalar", s_ve)
                tdst = lnsc["mu_b"][:, 0:512] if bu == 0 else lnsc["rs_b"][:, 0:512]
                ctx.op("scalar", "activation", tdst, B[2 * bu].ap(128, 512), AF.Relu,
                       inc=[(s_sc, 1)])
                B[2 * bu].set_guard(s_sc)
                ctx.wait("vector", s_sc)
                ctx.op("vector", "tensor_mul", kkb[:, oc * 512:(oc + 1) * 512],
                       tdst, tdst, inc=[(s_ve, 1)])
            kk_rdy = s_ve.n
            wsV = WStream(inp["r_fwvrx"], DH + D, stack, stg_cols=2048, nbuf=1)
            for f in range(8):
                bu = 0
                wsV.load(f, 0)
                wb = wsV.wb[bu]
                ctx.wait("tensor", s_ve, wsV.ready[bu])
                ctx.wait("tensor", s_ve, kk_rdy)
                mm_group(B[0], lambda k, _w=wb: _w[:, k * 128:(k + 1) * 128],
                         lambda k: kkb[:, k * 512:(k + 1) * 512], 32, 512)
                mm_group(B[1], lambda k, _w=wb: _w[:, DH + k * 128:DH + (k + 1) * 128],
                         lambda k: grb[k][:], 8, 512)
                wsV.used(bu)
                ctx.wait("scalar", s_mm)
                ctx.wait("scalar", s_ve)
                tdst = lnsc["mu_b"][:, 0:512]
                ctx.op("scalar", "activation", tdst, B[1].ap(128, 512),
                       AF.Sigmoid, inc=[(s_sc, 1)])
                B[1].set_guard(s_sc)
                ctx.wait("vector", s_sc)
                ctx.op("vector", "tensor_mul", lnsc["t0"][:, 0:512],
                       B[0].ap(128, 512), tdst)
                ctx.op("vector", "tensor_add", r_t[f][:, c0:c0 + 512],
                       lnsc["t0"][:, 0:512], r_t[f][:, c0:c0 + 512], inc=[(s_ve, 1)])
                B[0].set_guard(s_ve)

        hce = ctx.sbuf("hce", [128, 8], F32, stack=phE, side="right")
        phE0 = ExitStack()
        tmp0 = ctx.sbuf("e0tmp0", [128, 8 * 512], F32, stack=phE0, side="right")
        tmph = [ctx.sbuf(f"e0h{f}", [128, 512], F32, stack=phE0, side="right")
                for f in range(8)]
        emit_ln([r_t[f][:, NT - 512:NT] for f in range(8)], 512,
                [tmp0[:, f * 512:(f + 1) * 512] for f in range(8)],
                lnsc, B[4], B[5], B[3])
        emit_ln([tmp0[:, f * 512:(f + 1) * 512] for f in range(8)], 512,
                [tmph[f][:] for f in range(8)], lnsc, B[4], B[5], B[3],
                gamma=smalls["r_ln1w"], beta=smalls["r_ln1b"])
        for f in range(8):
            ctx.op("vector", "tensor_copy", out=hce[:, f:f + 1],
                   in_=tmph[f][:, 511:512], inc=([(s_ve, 1)] if f == 7 else None))
        phE0.close()
        c1_done = [allgather_prev(hce[:], 8, recvh[:])]
        if "erecvh" in debug:
            t = dbg_out("erecvh", (128, 24))
            ctx.wait("gpsimd", s_gd)
            ctx.wait("gpsimd", s_ve)
            ctx.dma("gpsimd", t[:, 0:8], recvh[:], s_st)
            ctx.dma("gpsimd", t[:, 8:16], hce[:], s_st)
            ctx.dma("gpsimd", t[:, 16:24], hce[:], s_st)

        run_beta = SUBE in ("beta", "gamma", "full")
        phB2 = ExitStack()
        wsKVR = WStream(inp["r_wkvrx"], 3 * D, phB2, stg_cols=2048)
        for q in range(NQ if run_beta else 0):
            phQ2 = ExitStack()
            xkb = [ctx.sbuf(f"xkbB{q}_{f}", [128, TQE], BF16, stack=phQ2, side="right")
                   for f in range(8)]
            xvb = [ctx.sbuf(f"xvbB{q}_{f}", [128, TQE], BF16, stack=phQ2, side="right")
                   for f in range(8)]
            phQh = ExitStack()
            h0t = ctx.sbuf(f"h0tB{q}", [128, 8 * TQE], F32, stack=phQh, side="right")
            hb = [ctx.sbuf(f"hbB{q}_{f}", [128, TQE + 1], BF16, stack=phQh, side="right")
                  for f in range(8)]
            emit_h_quarter(q, "B", hb, h0t, False, xkb, xvb, None)
            mix_rdy = s_ve.n
            phQh.close()
            phQs = ExitStack()
            ekE = ctx.sbuf(f"ekB{q}", [128, TQE], F32, stack=phQs, side="right")
            ekvE = ctx.sbuf(f"ekvB{q}", [128, TQE], F32, stack=phQs, side="right")
            ewrev = ctx.sbuf(f"ewrevB{q}", [128, TQE], F32, stack=phQs, side="right")
            ttE = ctx.sbuf(f"ttBq{q}", [128, TQE], F32, stack=phQs, side="right")
            wsKVR.load(0, 0)
            for f in range(8):
                bu = f % 2
                if f < 7:
                    wsKVR.load(f + 1, 1 - bu)
                wb = wsKVR.wb[bu]
                ctx.wait("tensor", s_ve, wsKVR.ready[bu])
                ctx.wait("tensor", s_ve, mix_rdy)
                B[4].mm_wait()
                ctx.wait("tensor", s_sc)
                ctx.wait("tensor", s_ld)
                ctx.op("tensor", "matmul", B[4].ap(128, TQE),
                       wrow[:, f * 128:(f + 1) * 128], iotar[:, 0:TQE],
                       start=True, stop=True, inc=[(s_mm, 1)])
                ctx.wait("scalar", s_mm)
                ctx.op("scalar", "activation", ewrev[:], B[4].ap(128, TQE), AF.Exp,
                       inc=[(s_sc, 1)])
                B[4].set_guard(s_sc)
                mm_group(B[0], lambda k, _w=wb: _w[:, k * 128:(k + 1) * 128],
                         lambda k: xkb[k][:], 8, TQE)
                mm_group(B[2], lambda k, _w=wb: _w[:, D + k * 128:D + (k + 1) * 128],
                         lambda k: xvb[k][:], 8, TQE)
                ctx.wait("scalar", s_mm)
                ctx.op("scalar", "activation", ekE[:], B[0].ap(128, TQE), AF.Exp,
                       inc=[(s_sc, 1)])
                B[0].set_guard(s_sc)
                ctx.wait("vector", s_sc)
                ctx.op("vector", "tensor_mul", ekvE[:], B[2].ap(128, TQE), ekE[:],
                       inc=[(s_ve, 1)])
                B[2].set_guard(s_ve)
                ctx.op("vector", "tensor_mul", ttE[:], ekE[:], ewrev[:])
                ctx.op("vector", "reduce_sum", dB[q][:, f:f + 1], ttE[:], axis=AX.X)
                ctx.op("vector", "tensor_mul", ttE[:], ekvE[:], ewrev[:])
                ctx.op("vector", "reduce_sum", dA[q][:, f:f + 1], ttE[:], axis=AX.X,
                       inc=[(s_ve, 1)])
            phQs.close()
            phQ2.close()
        phB2.close()
        ctx.wait("vector", s_ve)
        for st, ds in (((0, dA), (8, dB)) if run_beta else ()):
            for f in range(8):
                ctx.op("vector", "scalar_tensor_tensor", out=sendAB[:, st + f:st + f + 1],
                       in0=ds[0][:, f:f + 1], scalar=ewq_col[:, f:f + 1],
                       in1=ds[1][:, f:f + 1], op0=ALU.mult, op1=ALU.add)
                ctx.op("vector", "scalar_tensor_tensor", out=sendAB[:, st + f:st + f + 1],
                       in0=sendAB[:, st + f:st + f + 1], scalar=ewq_col[:, f:f + 1],
                       in1=ds[2][:, f:f + 1], op0=ALU.mult, op1=ALU.add)
                ctx.op("vector", "scalar_tensor_tensor", out=sendAB[:, st + f:st + f + 1],
                       in0=sendAB[:, st + f:st + f + 1], scalar=ewq_col[:, f:f + 1],
                       in1=ds[3][:, f:f + 1], op0=ALU.mult, op1=ALU.add,
                       inc=([(s_ve, 1)] if f == 7 else None))
        c2_done = allgather_prev(sendAB[:], 16, recvAB[:]) if run_beta else 0
        if "edots" in debug:
            t = dbg_out("edots", (128, 96))
            ctx.wait("gpsimd", s_ve)
            ctx.wait("gpsimd", s_gd)
            ctx.dma("gpsimd", t[:, 0:64], dots[:], s_st)
            ctx.dma("gpsimd", t[:, 64:80], sendAB[:], s_st)
            ctx.dma("gpsimd", t[:, 80:96], recvAB[:], s_st)

        run_gamma = SUBE in ("gamma", "full")
        for q in range(NQ if run_gamma else 0):
            c0 = q * TQE
            phQ = ExitStack()
            rwkvb = [ctx.sbuf(f"rwkvG{q}_{f}", [128, TQE], BF16, stack=phQ, side="right")
                     for f in range(8)]
            phQ2 = ExitStack()
            xkb = [ctx.sbuf(f"xkbG{q}_{f}", [128, TQE], BF16, stack=phQ2, side="right")
                   for f in range(8)]
            xvb = [ctx.sbuf(f"xvbG{q}_{f}", [128, TQE], BF16, stack=phQ2, side="right")
                   for f in range(8)]
            xrb = [ctx.sbuf(f"xrbG{q}_{f}", [128, TQE], BF16, stack=phQ2, side="right")
                   for f in range(8)]
            phQh = ExitStack()
            h0t = ctx.sbuf(f"h0tG{q}", [128, 8 * TQE], F32, stack=phQh, side="right")
            hb = [ctx.sbuf(f"hbG{q}_{f}", [128, TQE + 1], BF16, stack=phQh, side="right")
                  for f in range(8)]
            emit_h_quarter(q, "G", hb, h0t, True, xkb, xvb, xrb)
            mix_rdy = s_ve.n
            phQh.close()
            phQs = ExitStack()
            wsKVR = WStream(inp["r_wkvrx"], 3 * D, phQs, stg_cols=2048)
            ekE = ctx.sbuf(f"ekG{q}", [128, TQE], F32, stack=phQs, side="right")
            ekvE = ctx.sbuf(f"ekvG{q}", [128, TQE], F32, stack=phQs, side="right")
            eukE = ctx.sbuf(f"eukG{q}", [128, TQE], F32, stack=phQs, side="right")
            evtE = ctx.sbuf(f"evtG{q}", [128, TQE], F32, stack=phQs, side="right")
            ewbE = ctx.sbuf(f"ewbG{q}", [128, TQE], F32, stack=phQs, side="right")
            Abuf = ctx.sbuf(f"AbG{q}", [128, TQE + 1], F32, stack=phQs, side="right")
            Bbuf = ctx.sbuf(f"BbG{q}", [128, TQE + 1], F32, stack=phQs, side="right")
            srE = ctx.sbuf(f"srG{q}", [128, TQE], F32, stack=phQs, side="right")
            hopc = ctx.sbuf(f"hopG{q}", [128, 1], F32, stack=phQs, side="right")
            if q == 0:
                ctx.wait("vector", s_gd, c2_done)
            wsKVR.load(0, 0)
            for f in range(8):
                bu = f % 2
                if f < 7:
                    wsKVR.load(f + 1, 1 - bu)
                wb = wsKVR.wb[bu]
                ctx.wait("tensor", s_ve, wsKVR.ready[bu])
                ctx.wait("tensor", s_ve, mix_rdy)
                mm_group(B[0], lambda k, _w=wb: _w[:, k * 128:(k + 1) * 128],
                         lambda k: xkb[k][:], 8, TQE)
                mm_group(B[2], lambda k, _w=wb: _w[:, D + k * 128:D + (k + 1) * 128],
                         lambda k: xvb[k][:], 8, TQE)
                ctx.wait("scalar", s_mm)
                ctx.wait("scalar", s_ve)
                ctx.op("scalar", "activation", ekE[:], B[0].ap(128, TQE), AF.Exp,
                       inc=[(s_sc, 1)])
                B[0].set_guard(s_sc)
                ctx.wait("vector", s_sc)
                ctx.op("vector", "tensor_scalar_mul", eukE[:], ekE[:], eu_col[:, f:f + 1])
                ctx.op("vector", "tensor_mul", ekvE[:], B[2].ap(128, TQE), ekE[:])
                ctx.op("vector", "tensor_mul", evtE[:], B[2].ap(128, TQE), eukE[:],
                       inc=[(s_ve, 1)])
                B[2].set_guard(s_ve)
                if q == 0:
                    ctx.wait("scalar", s_gd, c2_done)
                ctx.op("scalar", "activation", ewbE[:], ekE[:], AF.Identity,
                       scale=0.0, bias=ew_col[:, f:f + 1])
                ctx.op("scalar", "copy", Abuf[:, 0:1],
                       (recvAB[:, f:f + 1] if q == 0 else Afin[:, f:f + 1]))
                ctx.op("scalar", "copy", Bbuf[:, 0:1],
                       (recvAB[:, 8 + f:9 + f] if q == 0 else Bfin[:, f:f + 1]),
                       inc=[(s_sc, 1)])
                ctx.wait("tensor", s_ve, mix_rdy)
                mm_group(B[1], lambda k, _w=wb: _w[:, 2 * D + k * 128:2 * D + (k + 1) * 128],
                         lambda k: xrb[k][:], 8, TQE)
                ctx.wait("vector", s_sc)
                ctx.op("vector", "tensor_tensor_scan", out=Abuf[:, 1:TQE + 1],
                       data0=ewbE[:], data1=ekvE[:], initial=Abuf[:, 0:1],
                       op0=ALU.mult, op1=ALU.add)
                ctx.op("vector", "tensor_tensor_scan", out=Bbuf[:, 1:TQE + 1],
                       data0=ewbE[:], data1=ekE[:], initial=Bbuf[:, 0:1],
                       op0=ALU.mult, op1=ALU.add, inc=[(s_ve, 1)])
                ctx.wait("scalar", s_ve)
                ctx.wait("scalar", s_mm)
                ctx.op("scalar", "copy", Afin[:, f:f + 1], Abuf[:, TQE:TQE + 1])
                ctx.op("scalar", "copy", Bfin[:, f:f + 1], Bbuf[:, TQE:TQE + 1])
                ctx.op("scalar", "activation", srE[:], B[1].ap(128, TQE), AF.Sigmoid,
                       inc=[(s_sc, 1)])
                B[1].set_guard(s_sc)
                ctx.wait("vector", s_sc)
                ctx.op("vector", "tensor_add", eukE[:], Bbuf[:, 0:TQE], eukE[:])
                ctx.op("vector", "tensor_add", evtE[:], Abuf[:, 0:TQE], evtE[:],
                       inc=[(s_ve, 1)])
                ctx.wait("scalar", s_ve)
                ctx.op("scalar", "copy", hopc[:], eukE[:, 0:1], inc=[(s_sc, 1)])
                ctx.wait("vector", s_sc)
                ctx.op("vector", "reciprocal", ekE[:], eukE[:], inc=[(s_ve, 1)])
                ctx.wait("scalar", s_ve)
                ctx.op("scalar", "copy", hopc[:], ekE[:, 0:1], inc=[(s_sc, 1)])
                ctx.wait("vector", s_sc)
                ctx.op("vector", "tensor_mul", evtE[:], evtE[:], srE[:])
                ctx.op("vector", "tensor_mul", rwkvb[f][:], evtE[:], ekE[:],
                       inc=[(s_ve, 1)])
            phQs.close()
            phQ2.close()
            if q > 0:
                gsl = [ctx.sbuf(f"gslS{q}_{f}", [128, 513], BF16, stack=phQ, side="right")
                       for f in range(8)]
            phS1 = ExitStack()
            h0bs = [ctx.sbuf(f"h0bS{q}_{f}", [128, TQE], BF16, stack=phS1, side="right")
                    for f in range(8)]
            xfs = [ctx.sbuf(f"xfS{q}_{f}", [128, TQE], F32, stack=phS1, side="right")
                   for f in range(8)]
            wsSWO = WStream(inp["r_swox"], 2 * D, phS1)
            emit_ln([r_t[f][:, c0:c0 + TQE] for f in range(8)], TQE,
                    [h0bs[f][:] for f in range(8)], lnsc, B[4], B[5], B[3])
            h0_rdy = s_ve.n
            wsSWO.load(0, 0)
            for f in range(8):
                bu = f % 2
                if f < 7:
                    wsSWO.load(f + 1, 1 - bu)
                wb = wsSWO.wb[bu]
                ctx.wait("tensor", s_ve, wsSWO.ready[bu])
                ctx.wait("tensor", s_ve, h0_rdy)
                bk = B[bu]
                bk.mm_wait()
                for k in range(8):
                    ctx.op("tensor", "matmul", bk.ap(128, TQE),
                           wb[:, k * 128:(k + 1) * 128], h0bs[k][:],
                           start=(k == 0), stop=False)
                for k in range(8):
                    ctx.op("tensor", "matmul", bk.ap(128, TQE),
                           wb[:, D + k * 128:D + (k + 1) * 128], rwkvb[k][:],
                           start=False, stop=(k == 7),
                           inc=([(s_mm, 1)] if k == 7 else None))
                wsSWO.used(bu)
                ctx.wait("scalar", s_mm)
                ctx.op("scalar", "copy", xfs[f][:], bk.ap(128, TQE), inc=[(s_sc, 1)])
                bk.set_guard(s_sc)
            if q == 0 and "erwkv0" in debug:
                t = dbg_out("erwkv0", (8, 128, TQE))
                ctx.wait("gpsimd", s_ve)
                for f in range(8):
                    ctx.dma("gpsimd", t[f], rwkvb[f][:], s_st)
            if q == 0 and "exfn0" in debug:
                t = dbg_out("exfn0", (8, 128, TQE))
                ctx.wait("gpsimd", s_sc)
                for f in range(8):
                    ctx.dma("gpsimd", t[f], xfs[f][:], s_st)
            if q == 0:
                gdst = [g0b[f][:, 1:513] for f in range(8)]
            else:
                ctx.wait("vector", s_ve)
                for f in range(8):
                    ctx.op("vector", "tensor_copy", out=gsl[f][:, 0:1],
                           in_=g_last[:, f:f + 1])
                gdst = [gsl[f][:, 1:513] for f in range(8)]
            ctx.wait("scalar", s_sc)
            emit_ln([xfs[f][:] for f in range(8)], TQE, gdst,
                    lnsc, B[4], B[5], B[3],
                    gamma=smalls["r_ln2w"], beta=smalls["r_ln2b"])
            ctx.wait("vector", s_sc)
            ctx.wait("vector", s_ve)
            for f in range(8):
                ctx.op("vector", "tensor_add", r_t[f][:, c0:c0 + TQE],
                       xfs[f][:], r_t[f][:, c0:c0 + TQE],
                       inc=([(s_ve, 1)] if f == 7 else None))
            for f in range(8):
                ctx.op("vector", "tensor_copy", out=g_last[:, f:f + 1],
                       in_=gdst[f][:, 511:512], inc=([(s_ve, 1)] if f == 7 else None))
            phS1.close()
            if q > 0 and SUBE == "full":
                phase_waits(("sync",))
                emit_ffn_slice(c0, gsl, phQ)
            phQ.close()
        if SUBE == "full":
            ctx.wait("vector", s_ve)
            c3_done = allgather_prev(g_last[:], 8, recvh[:])
            phS = ExitStack()
            ctx.wait("vector", s_gd, c3_done)
            for f in range(8):
                ctx.op("vector", "tensor_copy", out=g0b[f][:, 0:1], in_=recvh[:, f:f + 1],
                       inc=([(s_ve, 1)] if f == 7 else None))
            phase_waits(("sync",))
            emit_ffn_slice(0, g0b, phS)
            phS.close()
        phE.close()

    if "r3" in debug:
        t = dbg_out("r3", (8, 128, NT))
        ctx.wait("gpsimd", s_ve)
        for f in range(8):
            ctx.dma("gpsimd", t[f], r_t[f][:], s_st)

    if max_phase >= 8:
        phF = ExitStack()
        phase_waits(("sync", "gpsimd", "scalar"))
        xlnF = [ctx.sbuf(f"xlnF{f}", [128, NT], BF16, stack=phF, side="right")
                for f in range(8)]
        gfm = ctx.sbuf("gfm", [8, NT], BF16, stack=phF, side="right")
        onecb = ctx.sbuf("onecb", [1, 128], BF16, stack=phF, side="right")
        ctx.op("vector", "memset", onecb[:], 1.0, inc=[(s_ve, 1)])
        phF1 = ExitStack()
        lnsc = make_ln_scratch(phF1)
        for sl in range(4):
            c0 = sl * 512
            emit_ln([r_t[f][:, c0:c0 + 512] for f in range(8)], 512,
                    [xlnF[f][:, c0:c0 + 512] for f in range(8)],
                    lnsc, B[4], B[5], B[3])
        wgb = ctx.sbuf("wgb", [128, 64], BF16, stack=phF1, side="right")
        bgb = ctx.sbuf("bgb", [128, 128], F32, stack=phF1, side="right")
        identF = ctx.sbuf("identF", [128, 128], F32, stack=phF1, side="right")
        gsc = ctx.sbuf("gsc", [128, 5 * 128], F32, stack=phF1, side="right")
        lgt = gsc[:, 0:128]
        p8 = gsc[:, 128:256]
        msk = gsc[:, 256:384]
        grows = gsc[:, 384:384 + 64]
        getm = gsc[:, 448:576]
        if wstage_guard[0] is not None:
            ctx.wait("sync", s_ve, wstage_guard[0])
        ctx.dma("sync", wstage[:, 0:64], inp["m_wgx"][:], s_ld)
        ctx.dma("sync", bgb[:], inp["m_bgx"][:], s_ld)
        ctx.dma("sync", identF[:], inp["ident"][:], s_ld)
        ctx.wait("vector", s_ld)
        ctx.op("vector", "tensor_copy", out=wgb[:], in_=wstage[:, 0:64], inc=[(s_ve, 1)])
        wstage_guard[0] = s_ve.n
        gb = B[6]
        gb.mm_wait()
        ctx.wait("tensor", s_ve)
        for tt in range(16):
            for k in range(8):
                ctx.op("tensor", "matmul", gb.ap(128, 8, off=tt * 8),
                       xlnF[k][:, tt * 128:(tt + 1) * 128], wgb[:, k * 8:(k + 1) * 8],
                       start=(tt == 0 and k == 0), stop=(tt == 15 and k == 7),
                       inc=([(s_mm, 1)] if (tt == 15 and k == 7) else None))
        ctx.wait("vector", s_mm)
        ctx.op("vector", "tensor_add", lgt[:], gb.ap(128, 128), bgb[:], inc=[(s_ve, 1)])
        gb.set_guard(s_ve)
        ctx.op("vector", "reduce_max",
               grows[:, 0:16], lgt.rearrange("p (t e) -> p t e", e=8), axis=AX.X,
               negate=True, inc=[(s_ve, 1)])
        ctx.wait("scalar", s_ve)
        for tt in range(16):
            ctx.op("scalar", "activation", p8[:, tt * 8:(tt + 1) * 8],
                   lgt[:, tt * 8:(tt + 1) * 8], AF.Exp,
                   bias=grows[:, tt:tt + 1], inc=([(s_sc, 1)] if tt == 15 else None))
        ctx.wait("vector", s_sc)
        ctx.op("vector", "reduce_sum",
               grows[:, 16:32], p8.rearrange("p (t e) -> p t e", e=8), axis=AX.X,
               inc=[(s_ve, 1)])
        ctx.wait("scalar", s_ve)
        ctx.op("scalar", "copy", grows[:, 32:48], grows[:, 16:32], inc=[(s_sc, 1)])
        ctx.wait("vector", s_sc)
        ctx.op("vector", "reciprocal", grows[:, 16:32], grows[:, 32:48],
               inc=[(s_ve, 1)])
        ctx.op("vector", "reduce_max",
               grows[:, 32:48], p8.rearrange("p (t e) -> p t e", e=8), axis=AX.X,
               inc=[(s_ve, 1)])
        ctx.wait("gpsimd", s_ve)
        for tt in range(16):
            ctx.op("gpsimd", "tensor_single_scalar", msk[:, tt * 8:(tt + 1) * 8],
                   p8[:, tt * 8:(tt + 1) * 8], grows[:, 32 + tt:33 + tt],
                   op=ALU.is_lt)
        ctx.op("gpsimd", "engine_nop", inc=[(s_gp, 1)])
        ctx.wait("vector", s_gp)
        ctx.op("vector", "tensor_mul", msk[:], p8[:], msk[:], inc=[(s_ve, 1)])
        ctx.op("vector", "reduce_max",
               grows[:, 48:64], msk.rearrange("p (t e) -> p t e", e=8), axis=AX.X,
               inc=[(s_ve, 1)])
        ctx.wait("gpsimd", s_ve)
        for tt in range(16):
            ctx.op("gpsimd", "tensor_single_scalar", msk[:, tt * 8:(tt + 1) * 8],
                   p8[:, tt * 8:(tt + 1) * 8], grows[:, 48 + tt:49 + tt],
                   op=ALU.is_ge)
        ctx.op("gpsimd", "engine_nop", inc=[(s_gp, 1)])
        ctx.wait("vector", s_gp)
        ctx.op("vector", "tensor_mul", getm[:], p8[:], msk[:])
        for tt in range(16):
            ctx.op("vector", "tensor_scalar_mul", getm[:, tt * 8:(tt + 1) * 8],
                   getm[:, tt * 8:(tt + 1) * 8], grows[:, 16 + tt:17 + tt],
                   inc=([(s_ve, 1)] if tt == 15 else None))
        for tt in range(16):
            tb = B[7]
            tb.mm_wait()
            ctx.wait("tensor", s_ve)
            ctx.op("tensor", "transpose", tb.ap(8, 128), getm[:, tt * 8:(tt + 1) * 8],
                   identF[:], inc=[(s_mm, 1)])
            ctx.wait("scalar", s_mm)
            ctx.op("scalar", "copy", gfm[0:8, tt * 128:(tt + 1) * 128], tb.ap(8, 128),
                   inc=[(s_sc, 1)])
            tb.set_guard(s_sc)
        if "egate" in debug:
            t = dbg_out("egate", (8, NT))
            ctx.wait("gpsimd", s_sc)
            ctx.dma("gpsimd", t[:], gfm[:], s_st)
        phF1.close()

        TM = NT // 2
        gebF = ctx.sbuf("gebF", [128, TM], BF16, stack=phF, side="right")
        gerow = ctx.sbuf("gerow", [1, NT], BF16, stack=phF, side="right")
        HqF = ctx.sbuf("HqF", [128, 32 * TM], BF16, stack=phF, side="right")
        tmpF = [ctx.sbuf(f"tmpF{i}", [128, 512], F32, stack=phF, side="right")
                for i in range(2)]
        b1F = ctx.sbuf("b1F", [128, NE * 32], F32, stack=phF, side="right")
        b2F = ctx.sbuf("b2F", [128, NE * 8], F32, stack=phF, side="right")
        ctx.dma("sync", b1F[:], inp["m_b1x"][:], s_ld)
        ctx.dma("sync", b2F[:], inp["m_b2x"][:], s_ld)
        wsW1 = WStream(inp["m_w1x"][0], D, phF, stg_cols=512)
        wsW2 = WStream(inp["m_w2x"][0], DH, phF, stg_cols=1024)
        ctx.wait("tensor", s_ld)
        xln_rdy = s_ve.n
        ge_guard = [None]
        for e in range(NE):
            wsW1.dram = inp["m_w1x"][e]
            wsW2.dram = inp["m_w2x"][e]
            ctx.wait("gpsimd", s_sc)
            ctx.dma("gpsimd", gerow[:], gfm[e:e + 1, :], s_gd)
            for th in (0, 1):
                t0 = th * TM
                bb = B[6]
                bb.mm_wait()
                B[7].mm_wait()
                ctx.wait("tensor", s_gd)
                ctx.op("tensor", "matmul", bb.ap(128, 512, off=0), onecb[:],
                       gerow[:, t0:t0 + 512], start=True, stop=True)
                ctx.op("tensor", "matmul", PSD[0:128, 0:512], onecb[:],
                       gerow[:, t0 + 512:t0 + 1024], start=True, stop=True,
                       inc=[(s_mm, 1)])
                ctx.wait("scalar", s_mm)
                ctx.wait("scalar", s_ve)
                ctx.op("scalar", "copy", gebF[:, 0:512], bb.ap(128, 512))
                ctx.op("scalar", "copy", gebF[:, 512:1024], PSD[0:128, 0:512],
                       inc=[(s_sc, 1)])
                bb.set_guard(s_sc)
                B[7].set_guard(s_sc)
                wsW1.load(0, 0)
                for oc in range(32):
                    bu = oc % 2
                    if oc < 31:
                        wsW1.load(oc + 1, 1 - bu)
                    wb = wsW1.wb[bu]
                    ctx.wait("tensor", s_ve, wsW1.ready[bu])
                    ctx.wait("tensor", s_ve, xln_rdy)
                    for i in range(2):
                        mm_group(B[2 * bu + i],
                                 lambda k, _w=wb: _w[:, k * 128:(k + 1) * 128],
                                 lambda k, _c=t0 + i * 512: xlnF[k][:, _c:_c + 512],
                                 8, 512)
                    wsW1.used(bu)
                    ctx.wait("scalar", s_mm)
                    for i in range(2):
                        ctx.op("scalar", "activation",
                               HqF[:, oc * TM + i * 512:oc * TM + (i + 1) * 512],
                               B[2 * bu + i].ap(128, 512), AF.Gelu,
                               bias=b1F[:, e * 32 + oc:e * 32 + oc + 1],
                               inc=([(s_sc, 1)] if i == 1 else None))
                        B[2 * bu + i].set_guard(s_sc)
                hq_rdy = s_sc.n
                wsW2.load(0, 0)
                for f in range(8):
                    bu = f % 2
                    if f < 7:
                        wsW2.load(f + 1, 1 - bu)
                    wb = wsW2.wb[bu]
                    ctx.wait("tensor", s_ve, wsW2.ready[bu])
                    ctx.wait("tensor", s_sc, hq_rdy)
                    for i in range(2):
                        mm_group(B[2 * bu + i],
                                 lambda k, _w=wb: _w[:, k * 128:(k + 1) * 128],
                                 lambda k, _i=i: HqF[:, k * TM + _i * 512:k * TM + _i * 512 + 512],
                                 32, 512)
                    wsW2.used(bu)
                    ctx.wait("scalar", s_mm)
                    ctx.wait("scalar", s_ve)
                    for i in range(2):
                        ctx.op("scalar", "activation", tmpF[i][:],
                               B[2 * bu + i].ap(128, 512), AF.Identity,
                               bias=b2F[:, e * 8 + f:e * 8 + f + 1],
                               inc=([(s_sc, 1)] if i == 1 else None))
                        B[2 * bu + i].set_guard(s_sc)
                    ctx.wait("vector", s_sc)
                    for i in range(2):
                        c0 = t0 + i * 512
                        ctx.op("vector", "tensor_mul", tmpF[i][:], tmpF[i][:],
                               gebF[:, i * 512:(i + 1) * 512])
                        ctx.op("vector", "tensor_add", r_t[f][:, c0:c0 + 512],
                               tmpF[i][:], r_t[f][:, c0:c0 + 512],
                               inc=([(s_ve, 1)] if i == 1 else None))
        phF.close()

    ctx.wait("gpsimd", s_ve)
    for f in range(8):
        ctx.dma("gpsimd", out[f], r_t[f][:], s_st)


    ctx.wait("gpsimd", s_st)
    ctx.wait("sync", s_st)
    return



def _chunk_vec(v):
    d = v.shape[0]
    return np.ascontiguousarray(v.reshape(d // 128, 128).T)


def _rope_tables():
    pos = np.arange(S, dtype=np.float64)
    ids = np.arange(D // 2, dtype=np.float64)
    theta = (10000.0 ** (-2.0 * ids)) / (D // 2)
    r = pos[None, :] * theta[:, None]
    cosf = np.repeat(np.cos(r).astype(np.float32), 2, axis=0)
    sinf = np.repeat(np.sin(r).astype(np.float32), 2, axis=0)
    return cosf, sinf


def _perm_pm():
    P = np.zeros((128, 128), np.float32)
    for i in range(64):
        P[2 * i + 1, 2 * i] = -1.0
        P[2 * i, 2 * i + 1] = 1.0
    return P


def stage_inputs(inp):
    T = np.ascontiguousarray
    A = lambda k: np.asarray(inp[k], dtype=np.float32)
    cosf, sinf = _rope_tables()
    x = A("L0_attn_x")

    sh = {}
    sh["a_wdq"] = T(A("L0_attn_Wd_q_w"))
    sh["a_bdq"] = _chunk_vec(A("L0_attn_Wd_q_b"))
    sh["a_wuq"] = T(A("L0_attn_Wu_q_w"))
    sh["a_buq"] = _chunk_vec(A("L0_attn_Wu_q_b"))
    sh["a_wqr"] = T(A("L0_attn_Wq_r_w"))
    sh["a_bqr"] = _chunk_vec(A("L0_attn_Wq_r_b"))
    sh["a_wkr"] = T(A("L0_attn_Wk_r_w"))
    sh["a_bkr"] = _chunk_vec(A("L0_attn_Wk_r_b"))
    sh["a_wdkv"] = T(A("L0_attn_Wd_kv_w"))
    sh["a_bdkv"] = _chunk_vec(A("L0_attn_Wd_kv_b"))
    sh["a_wuk"] = T(A("L0_attn_Wu_k_w"))
    sh["a_buk"] = _chunk_vec(A("L0_attn_Wu_k_b"))
    sh["a_wuv"] = T(A("L0_attn_Wu_v_w"))
    sh["a_buv"] = T(A("L0_attn_Wu_v_b")[None, :])
    sh["ident"] = np.eye(128, dtype=np.float32)
    sh["a_wo"] = T(A("L0_attn_Wo_w"))
    sh["a_bo"] = _chunk_vec(A("L0_attn_Wo_b"))
    sh["perm"] = _perm_pm()

    def lhsT_oc(Wt):
        K, M = Wt.shape
        return T(Wt.reshape(K // 128, 128, M // 128, 128)
                 .transpose(2, 1, 0, 3).reshape(M // 128, 128, K))

    w12 = np.concatenate([lhsT_oc(A("L0_ffn_w1_w").T),
                          lhsT_oc(A("L0_ffn_w2_w").T)], axis=2)
    sh["f_w12x"] = T(w12)
    sh["f_s1"] = _chunk_vec(A("L0_ffn_w1_s"))
    sh["f_s2"] = _chunk_vec(A("L0_ffn_w2_s"))
    sh["f_h1x"] = lhsT_oc(A("L0_ffn_h1_w").T)
    sh["f_sh1"] = _chunk_vec(A("L0_ffn_h1_s"))
    sh["f_h2x"] = lhsT_oc(A("L0_ffn_h2_w").T)
    sh["f_sh2"] = _chunk_vec(A("L0_ffn_h2_s"))
    sh["f_w3x"] = lhsT_oc(A("L0_ffn_w3_w").T)
    sh["f_s3"] = _chunk_vec(A("L0_ffn_w3_s"))

    sh["r_ln1w"] = _chunk_vec(A("L1_attn_ln1_w"))
    sh["r_ln1b"] = _chunk_vec(A("L1_attn_ln1_b"))
    sh["r_ln2w"] = _chunk_vec(A("L1_attn_ln2_w"))
    sh["r_ln2b"] = _chunk_vec(A("L1_attn_ln2_b"))
    sh["r_decay"] = _chunk_vec(A("L1_attn_att_time_decay"))
    sh["r_decay_row"] = T(A("L1_attn_att_time_decay")[None, :])
    sh["r_first"] = _chunk_vec(A("L1_attn_att_time_first"))
    sh["r_mixk"] = _chunk_vec(A("L1_attn_att_mix_k"))
    sh["r_mixv"] = _chunk_vec(A("L1_attn_att_mix_v"))
    sh["r_mixr"] = _chunk_vec(A("L1_attn_att_mix_r"))
    sh["r_wkvrx"] = T(np.concatenate([lhsT_oc(A("L1_attn_att_Wk").T),
                                      lhsT_oc(A("L1_attn_att_Wv").T),
                                      lhsT_oc(A("L1_attn_att_Wr").T)], axis=2))
    sh["r_swox"] = T(np.concatenate([lhsT_oc(A("L1_attn_short_W").T),
                                     lhsT_oc(A("L1_attn_att_Wo").T)], axis=2))
    sh["r_fmixk"] = _chunk_vec(A("L1_attn_ffn_mix_k"))
    sh["r_fmixr"] = _chunk_vec(A("L1_attn_ffn_mix_r"))
    sh["r_fwkx"] = lhsT_oc(A("L1_attn_ffn_Wk").T)
    sh["r_fwvrx"] = T(np.concatenate([lhsT_oc(A("L1_attn_ffn_Wv").T),
                                      lhsT_oc(A("L1_attn_ffn_Wr").T)], axis=2))

    wg = A("L1_ffn_Wg")
    sh["m_wgx"] = T(wg.reshape(8, 128, 8).transpose(1, 0, 2).reshape(128, 64))
    sh["m_bgx"] = T(np.tile(A("L1_ffn_bg"), (128, 16)))
    w1 = A("L1_ffn_W1")
    sh["m_w1x"] = T(w1.reshape(8, 8, 128, 32, 128).transpose(0, 3, 2, 1, 4)
                    .reshape(8, 32, 128, 1024))
    b1 = A("L1_ffn_b1")
    sh["m_b1x"] = T(np.concatenate([_chunk_vec(b1[e]) for e in range(NE)], 1))
    w2 = A("L1_ffn_W2")
    sh["m_w2x"] = T(w2.reshape(8, 32, 128, 8, 128).transpose(0, 3, 2, 1, 4)
                    .reshape(8, 8, 128, 4096))
    b2 = A("L1_ffn_b2")
    sh["m_b2x"] = T(np.concatenate([_chunk_vec(b2[e]) for e in range(NE)], 1))
    sh["iota_rev"] = T(np.arange(511, -1, -1, dtype=np.float32)[None, :])

    in_maps = []
    for c in range(NCORES):
        b, hh = c // 2, c % 2
        own = x[b, hh * NT:(hh + 1) * NT]
        oth = x[b, (1 - hh) * NT:(1 - hh) * NT + NT]
        xT = np.concatenate([own, oth], 0).T.astype(np.float32)
        posk = np.concatenate([np.arange(hh * NT, (hh + 1) * NT),
                               np.arange((1 - hh) * NT, (1 - hh) * NT + NT)])
        m = dict(sh)
        m["x_phA"] = T(xT.reshape(8, 128, S))
        m["x_own"] = T(xT[:, 0:NT])
        m["cosk"] = T(cosf[:, posk])
        m["sink"] = T(sinf[:, posk])
        mask = np.zeros((8, 1), np.float32)
        if hh == 1:
            mask[c - 1, 0] = 1.0
        m["mask8"] = mask
        in_maps.append(m)
    return in_maps


def kernel(**inputs):
    in_maps = stage_inputs(inputs)
    nc = build()
    used = nc._used_inputs
    in_maps = [{k: v for k, v in m.items() if k in used} for m in in_maps]
    res = run_bass_kernel_spmd(nc, in_maps, core_ids=list(range(NCORES)))
    outp = np.empty((4, S, D), np.float32)
    for c in range(NCORES):
        b, hh = c // 2, c % 2
        r = res.results[c]["out"]
        outp[b, hh * NT:(hh + 1) * NT] = r.reshape(D, NT).T
    return outp
